# version 06
# baseline (speedup 1.0000x reference)
import numpy as np
import ml_dtypes
from contextlib import ExitStack

import concourse.bass as bass
import concourse.mybir as mybir
from concourse.bass_utils import run_bass_kernel_spmd

F32 = mybir.dt.float32
BF16 = mybir.dt.bfloat16
FP8 = mybir.dt.float8e4
I8 = mybir.dt.int8
AF = mybir.ActivationFunctionType
ALU = mybir.AluOpType
AX = mybir.AxisListType

T = 2048
D = 2048
NH = 16
HS = 128
NI = 6144
KD = D // 128
KI = NI // 128
NG = 12
S_ATTN = float(1.0 / np.sqrt(np.float64(HS)))

C_S, C_NRD, C_RD, C_EPS, NCONST = 0, 1, 2, 3, 4


def _ds(t, r0, nr, width, c0=0, ncols=None):
    if ncols is None:
        ncols = width
    return bass.AP(t, r0 * width + c0, [[width, nr], [1, ncols]])


def _d3(t, width, r0, nk, c0, ncols, kstride=128):
    return bass.AP(t, r0 * width + c0, [[width, 128], [kstride * width, nk], [1, ncols]])


class _P:

    def __init__(self):
        self.s = {k: [] for k in ("sync", "tensor", "vector", "scalar", "gpsimd")}

    def em(self, eng, fn):
        self.s[eng].append(fn)

    def wait(self, eng, sem, val):
        if val > 0:
            self.s[eng].append(lambda e, s=sem, v=val: e.wait_ge(s, v))

    def install(self, nc):
        with nc.Block() as blk:
            self.install_into(blk)

    def install_into(self, blk):
        for name, fns in self.s.items():
            if fns:
                getattr(blk, name)(lambda e, fns=fns: [f(e) for f in fns])


def _mm(ps_ap, lh, rh, st, sp, sem=None):
    if sem is None:
        return lambda e, o=ps_ap, a=lh, b=rh, st=st, sp=sp: e.matmul(o, a, b, start=st, stop=sp)
    return lambda e, o=ps_ap, a=lh, b=rh, st=st, sp=sp, s=sem: e.matmul(
        o, a, b, start=st, stop=sp).then_inc(s, 1)


def build_nc(upto=4, dbg=False):
    nc = bass.Bass("TRN2", target_bir_lowering=False, detect_race_conditions=False)

    x_in = nc.dram_tensor("x", [T, D], F32, kind="ExternalInput")
    wq = nc.dram_tensor("wq", [D, D], F32, kind="ExternalInput")
    wk = nc.dram_tensor("wk", [D, D], F32, kind="ExternalInput")
    wv = nc.dram_tensor("wv", [D, D], F32, kind="ExternalInput")
    wo = nc.dram_tensor("wo", [D, D], F32, kind="ExternalInput")
    bqkv = nc.dram_tensor("bqkv", [128, 3 * KD], F32, kind="ExternalInput")
    bo_row = nc.dram_tensor("bo_row", [1, D], F32, kind="ExternalInput")
    gw = nc.dram_tensor("gw", [D, NI], F32, kind="ExternalInput")
    uw = nc.dram_tensor("uw", [D, NI], F32, kind="ExternalInput")
    dw = nc.dram_tensor("dw", [NI, D], F32, kind="ExternalInput")
    gs_in = nc.dram_tensor("gsc", [128, KI], F32, kind="ExternalInput")
    us_in = nc.dram_tensor("usc", [128, KI], F32, kind="ExternalInput")
    ds_in = nc.dram_tensor("dsc", [128, KD], F32, kind="ExternalInput")
    idf_in = nc.dram_tensor("idf", [128, 128], F32, kind="ExternalInput")
    idb_in = nc.dram_tensor("idb", [128, 128], BF16, kind="ExternalInput")
    mask_in = nc.dram_tensor("maskneg", [128, 128], F32, kind="ExternalInput")
    ones_row_in = nc.dram_tensor("ones_row", [1, 128], F32, kind="ExternalInput")
    cvec_in = nc.dram_tensor("cvec", [128, NCONST], F32, kind="ExternalInput")

    y_out = nc.dram_tensor("y", [T, D], F32, kind="ExternalOutput")

    sk = "ExternalOutput" if dbg else "Internal"
    qh = nc.dram_tensor("qh", [D, T], BF16, kind=sk)
    ql = nc.dram_tensor("ql", [D, T], BF16, kind=sk)
    kh = nc.dram_tensor("kh", [D, T], BF16, kind=sk)
    kl = nc.dram_tensor("kl", [D, T], BF16, kind=sk)
    vh = nc.dram_tensor("vh", [D, T], BF16, kind=sk)
    vl = nc.dram_tensor("vl", [D, T], BF16, kind=sk)
    ah = nc.dram_tensor("ah", [D, T], BF16, kind=sk)
    al = nc.dram_tensor("al", [D, T], BF16, kind=sk)
    x1f = nc.dram_tensor("x1f", [T, D], F32, kind=sk)
    pdbg = nc.dram_tensor("pdbg", [128, 8 * T], F32, kind="ExternalOutput") if dbg >= 2 else None
    vtdbg = nc.dram_tensor("vtdbg", [128, 1024], BF16, kind="ExternalOutput") if dbg >= 2 else None
    stdbg = nc.dram_tensor("stdbg", [128, 32], F32, kind="ExternalOutput") if dbg >= 2 else None
    accdbg = nc.dram_tensor("accdbg", [128, 16 * 1024], F32, kind="ExternalOutput") if dbg >= 2 else None
    xftdbg = nc.dram_tensor("xftdbg", [128, D], F32, kind="ExternalOutput") if dbg >= 2 else None
    ytmdbg = nc.dram_tensor("ytmdbg", [128, D], F32, kind="ExternalOutput") if dbg >= 2 else None
    ptdbg = nc.dram_tensor("ptdbg", [128, 1024], BF16, kind="ExternalOutput") if dbg >= 2 else None
    x1h = nc.dram_tensor("x1h", [D, T], BF16, kind=sk)
    x1l = nc.dram_tensor("x1l", [D, T], BF16, kind=sk)

    ctx = nc.ctx

    idf = ctx.enter_context(nc.sbuf_tensor("c_idf", [128, 128], F32))
    idb = ctx.enter_context(nc.sbuf_tensor("c_idb", [128, 128], BF16))
    masksb = ctx.enter_context(nc.sbuf_tensor("c_mask", [128, 128], F32))
    onesr = ctx.enter_context(nc.sbuf_tensor("c_onesr", [1, 128], F32))
    cvec = ctx.enter_context(nc.sbuf_tensor("c_cvec", [128, NCONST], F32))
    bqkv_sb = ctx.enter_context(nc.sbuf_tensor("c_bqkv", [128, 3 * KD], F32))
    bor_sb = ctx.enter_context(nc.sbuf_tensor("c_bor", [1, D], F32))
    gssb = ctx.enter_context(nc.sbuf_tensor("c_gs", [128, KI], F32))
    ussb = ctx.enter_context(nc.sbuf_tensor("c_us", [128, KI], F32))
    dssb = ctx.enter_context(nc.sbuf_tensor("c_ds", [128, KD], F32))

    s_col = cvec[:, C_S:C_S + 1]
    nrd_col = cvec[:, C_NRD:C_NRD + 1]
    rd_col = cvec[:, C_RD:C_RD + 1]
    eps_col = cvec[:, C_EPS:C_EPS + 1]

    s_c = ctx.enter_context(nc.semaphore("s_c"))
    sDBG = ctx.enter_context(nc.semaphore("sDBG")) if dbg >= 2 else None
    p0 = _P()
    consts = [(idf, idf_in), (idb, idb_in), (masksb, mask_in), (onesr, ones_row_in),
              (cvec, cvec_in), (bqkv_sb, bqkv), (bor_sb, bo_row),
              (gssb, gs_in), (ussb, us_in), (dssb, ds_in)]
    for dst, src in consts:
        p0.em("sync", lambda e, d=dst, s=src: e.dma_start(out=d[:], in_=s[:]).then_inc(s_c, 16))
    p0.wait("sync", s_c, 16 * len(consts))
    p0.install(nc)

    with ExitStack() as ph1:
        xhi = ph1.enter_context(nc.sbuf_tensor("xhi", [128, KD * T], BF16))
        xlo = ph1.enter_context(nc.sbuf_tensor("xlo", [128, KD * T], BF16))
        with ExitStack() as sc:
            xtok = [sc.enter_context(nc.sbuf_tensor(f"xtok{i}", [128, D], F32)) for i in range(2)]
            ptr = [sc.enter_context(nc.psum_tensor(f"p1tr{i}", [128, 128], F32)) for i in range(2)]
            sA_x = sc.enter_context(nc.semaphore("sA_x"))
            sA_tr = sc.enter_context(nc.semaphore("sA_tr"))
            sA_ev = sc.enter_context(nc.semaphore("sA_ev"))
            p = _P()
            for t in range(16):
                p.wait("sync", sA_tr, 16 * (t - 1))
                p.em("sync", lambda e, t=t: e.dma_start(
                    out=xtok[t % 2][:], in_=_ds(x_in, t * 128, 128, D)).then_inc(sA_x, 16))
            for t in range(16):
                p.wait("tensor", sA_x, 16 * (t + 1))
                for f in range(16):
                    idx = t * 16 + f
                    p.wait("tensor", sA_ev, idx - 1)
                    p.em("tensor", lambda e, t=t, f=f, idx=idx: e.transpose(
                        ptr[idx % 2][:], xtok[t % 2][:, f * 128:(f + 1) * 128],
                        idf[:]).then_inc(sA_tr, 1))
            for idx in range(256):
                t, f = divmod(idx, 16)
                dst = slice(f * T + t * 128, f * T + (t + 1) * 128)
                p.wait("vector", sA_tr, idx + 1)
                p.em("vector", lambda e, idx=idx, dst=dst: e.tensor_copy(xhi[:, dst], ptr[idx % 2][:]))
                p.em("vector", lambda e, idx=idx, dst=dst: e.tensor_tensor(
                    xlo[:, dst], ptr[idx % 2][:], xhi[:, dst], op=ALU.subtract).then_inc(sA_ev, 1))
            p.install(nc)

        with ExitStack() as sc:
            wstage = [sc.enter_context(nc.sbuf_tensor(f"wst{i}", [128, KD * 128], F32)) for i in range(2)]
            whi = [sc.enter_context(nc.sbuf_tensor(f"whi{i}", [128, KD * 128], BF16)) for i in range(2)]
            wlo = [sc.enter_context(nc.sbuf_tensor(f"wlo{i}", [128, KD * 128], BF16)) for i in range(2)]
            sthi = [sc.enter_context(nc.sbuf_tensor(f"sthi{i}", [128, T], BF16)) for i in range(2)]
            stlo = [sc.enter_context(nc.sbuf_tensor(f"stlo{i}", [128, T], BF16)) for i in range(2)]
            psB = [sc.enter_context(nc.psum_tensor(f"p1mm{i}", [128, 512], F32)) for i in range(2)]
            sB_w = sc.enter_context(nc.semaphore("sB_w"))
            sB_sp = sc.enter_context(nc.semaphore("sB_sp"))
            sB_mm = sc.enter_context(nc.semaphore("sB_mm"))
            sB_eh = sc.enter_context(nc.semaphore("sB_eh"))
            sB_el = sc.enter_context(nc.semaphore("sB_el"))
            sB_o = sc.enter_context(nc.semaphore("sB_o"))
            p = _P()
            mats = [(wq, qh, ql), (wk, kh, kl), (wv, vh, vl)]

            for idx in range(48):
                mat, wc = divmod(idx, 16)
                p.wait("gpsimd", sB_sp, idx - 1)
                p.em("gpsimd", lambda e, w_t=mats[mat][0], wc=wc, idx=idx: e.dma_start(
                    out=wstage[idx % 2][:], in_=_d3(w_t, D, 0, KD, wc * 128, 128)).then_inc(sB_w, 16))

            def emit_split(idx):
                p.wait("vector", sB_w, 16 * (idx + 1))
                p.wait("vector", sB_mm, 4 * (idx - 1))
                p.em("vector", lambda e, i=idx: e.tensor_copy(whi[i % 2][:], wstage[i % 2][:]))
                p.em("vector", lambda e, i=idx: e.tensor_tensor(
                    wlo[i % 2][:], wstage[i % 2][:], whi[i % 2][:], op=ALU.subtract).then_inc(sB_sp, 1))

            def emit_evacs(idx):
                mat, wc = divmod(idx, 16)
                b_ap = bqkv_sb[:, mat * KD + wc:mat * KD + wc + 1]
                for tc in range(4):
                    tile = idx * 4 + tc
                    p.wait("vector", sB_eh, tile + 1)
                    p.em("vector", lambda e, i=idx, tc=tc, tile=tile, b=b_ap: e.scalar_tensor_tensor(
                        stlo[i % 2][:, tc * 512:(tc + 1) * 512], psB[tile % 2][:], b,
                        sthi[i % 2][:, tc * 512:(tc + 1) * 512],
                        op0=ALU.add, op1=ALU.subtract).then_inc(sB_el, 1))

            emit_split(0)
            for idx in range(1, 48):
                emit_split(idx)
                emit_evacs(idx - 1)
            emit_evacs(47)

            for idx in range(48):
                p.wait("tensor", sB_sp, idx + 1)
                for tc in range(4):
                    tile = idx * 4 + tc
                    p.wait("tensor", sB_el, tile - 1)
                    for kc in range(16):
                        for ps in range(3):
                            lh = (whi if ps < 2 else wlo)[idx % 2][:, kc * 128:(kc + 1) * 128]
                            rh = (xhi if ps != 1 else xlo)[:, kc * T + tc * 512:kc * T + tc * 512 + 512]
                            last = (kc == 15 and ps == 2)
                            p.em("tensor", _mm(psB[tile % 2][:], lh, rh,
                                               kc == 0 and ps == 0, last, sB_mm if last else None))

            for tile in range(192):
                idx, tc = divmod(tile, 4)
                mat, wc = divmod(idx, 16)
                b_ap = bqkv_sb[:, mat * KD + wc:mat * KD + wc + 1]
                p.wait("scalar", sB_mm, tile + 1)
                if tc == 0:
                    p.wait("scalar", sB_o, 32 * (idx - 1))
                p.em("scalar", lambda e, i=idx, tc=tc, tile=tile, b=b_ap: e.activation(
                    out=sthi[i % 2][:, tc * 512:(tc + 1) * 512], in_=psB[tile % 2][:],
                    func=AF.Identity, bias=b, scale=1.0).then_inc(sB_eh, 1))

            for idx in range(48):
                mat, wc = divmod(idx, 16)
                p.wait("sync", sB_el, 4 * (idx + 1))
                p.em("sync", lambda e, i=idx, t=mats[mat][1], wc=wc: e.dma_start(
                    out=_ds(t, wc * 128, 128, T), in_=sthi[i % 2][:]).then_inc(sB_o, 16))
                p.em("sync", lambda e, i=idx, t=mats[mat][2], wc=wc: e.dma_start(
                    out=_ds(t, wc * 128, 128, T), in_=stlo[i % 2][:]).then_inc(sB_o, 16))
            p.wait("sync", sB_o, 16 * 96)
            p.install(nc)

    if upto < 2:
        return nc

    with ExitStack() as sc:
        qhb = [sc.enter_context(nc.sbuf_tensor(f"qhb{i}", [128, T], BF16)) for i in range(2)]
        qlb = [sc.enter_context(nc.sbuf_tensor(f"qlb{i}", [128, T], BF16)) for i in range(2)]
        khb = [sc.enter_context(nc.sbuf_tensor(f"khb{i}", [128, T], BF16)) for i in range(2)]
        klb = [sc.enter_context(nc.sbuf_tensor(f"klb{i}", [128, T], BF16)) for i in range(2)]
        vhb = [sc.enter_context(nc.sbuf_tensor(f"vhb{i}", [128, T], BF16)) for i in range(2)]
        vlb = [sc.enter_context(nc.sbuf_tensor(f"vlb{i}", [128, T], BF16)) for i in range(2)]
        P_sb = [sc.enter_context(nc.sbuf_tensor(f"Psb{i}", [128, T], F32)) for i in range(8)]
        vth = sc.enter_context(nc.sbuf_tensor("vth", [128, T], BF16))
        vtl = sc.enter_context(nc.sbuf_tensor("vtl", [128, T], BF16))
        vf32 = sc.enter_context(nc.sbuf_tensor("vf32", [128, 512], F32))
        pth = sc.enter_context(nc.sbuf_tensor("pth", [128, 2048], BF16))
        ptl = sc.enter_context(nc.sbuf_tensor("ptl", [128, 2048], BF16))
        asth = [sc.enter_context(nc.sbuf_tensor(f"asth{i}", [128, 1024], BF16)) for i in range(2)]
        astl = [sc.enter_context(nc.sbuf_tensor(f"astl{i}", [128, 1024], BF16)) for i in range(2)]
        nmax = sc.enter_context(nc.sbuf_tensor("nmax", [128, 8], F32))
        nmaxs = sc.enter_context(nc.sbuf_tensor("nmaxs", [128, 8], F32))
        srow = sc.enter_context(nc.sbuf_tensor("srow", [128, 8], F32))
        rrow = sc.enter_context(nc.sbuf_tensor("rrow", [128, 8], F32))
        mxp = sc.enter_context(nc.sbuf_tensor("mxp", [128, 4], F32))
        sH = sc.enter_context(nc.semaphore("sH"))
        sLG = sc.enter_context(nc.semaphore("sLG"))
        sMX = sc.enter_context(nc.semaphore("sMX"))
        sEX = sc.enter_context(nc.semaphore("sEX"))
        sPN = sc.enter_context(nc.semaphore("sPN"))
        sVF = sc.enter_context(nc.semaphore("sVF"))
        sTXt = sc.enter_context(nc.semaphore("sTXt"))
        sTH = sc.enter_context(nc.semaphore("sTH"))
        sTXe = sc.enter_context(nc.semaphore("sTXe"))
        sPV = sc.enter_context(nc.semaphore("sPV"))
        sAH = sc.enter_context(nc.semaphore("sAH"))
        sAE = sc.enter_context(nc.semaphore("sAE"))
        sAO = sc.enter_context(nc.semaphore("sAO"))
        sRC = sc.enter_context(nc.semaphore("sRC"))

        n_h = 0
        n_lgc = n_mx = n_ex = n_pn = n_rc = 0
        n_vf = 0
        n_tb = 0
        n_th = 0
        n_txe = 0
        n_pv = 0
        tb_after = {}

        blk2 = nc.Block()
        blk2.__enter__()
        for h in range(16):
            p = _P()
            p.wait("sync", sAE, 2 * (h - 1))
            for dst, src2 in [(qhb, qh), (qlb, ql), (khb, kh), (klb, kl), (vhb, vh), (vlb, vl)]:
                p.em("sync", lambda e, d=dst[h % 2], s=src2, h=h: e.dma_start(
                    out=d[:], in_=_ds(s, h * 128, 128, T)).then_inc(sH, 16))
            n_h += 6
            p.install_into(blk2)

            for g in range(2):
                gidx = 2 * h + g
                kbmax = g * 8 + 8
                with ExitStack() as hs:
                    p = _P()
                    lgA = hs.enter_context(nc.psum_tensor(f"lgA{gidx}", [128, T], F32))
                    lgB = hs.enter_context(nc.psum_tensor(f"lgB{gidx}", [128, T], F32))
                    exbase = n_ex
                    p.wait("tensor", sH, 16 * n_h)
                    p.wait("tensor", sAE, gidx)
                    if gidx >= 1:
                        p.wait("scalar", sTXt, tb_after[gidx - 1])
                    pend_norm = None
                    for qi in range(8):
                        qg = g * 8 + qi
                        ncols = (qg + 1) * 128
                        lg = lgA if qi % 2 == 0 else lgB
                        nchunks = (ncols + 511) // 512
                        p.wait("tensor", sEX, exbase + qi - 1)
                        ch_base = n_lgc
                        for cb in range(nchunks):
                            cw = min(512, ncols - cb * 512)
                            for ps in range(3):
                                lh = (qhb if ps != 1 else qlb)[h % 2][:, qg * 128:(qg + 1) * 128]
                                rh = (khb if ps != 2 else klb)[h % 2][:, cb * 512:cb * 512 + cw]
                                p.em("tensor", _mm(lg[:, cb * 512:cb * 512 + cw], lh, rh,
                                                   ps == 0, ps == 2, sLG if ps == 2 else None))
                            n_lgc += 1
                        for cb in range(nchunks):
                            cw = min(512, ncols - cb * 512)
                            p.wait("vector", sLG, ch_base + cb + 1)
                            if cb == nchunks - 1:
                                p.em("vector", lambda e, lg=lg, nc_=ncols: e.tensor_tensor(
                                    lg[:, nc_ - 128:nc_], lg[:, nc_ - 128:nc_], masksb[:], op=ALU.add))
                            p.em("vector", lambda e, lg=lg, cb=cb, cw=cw: e.reduce_max(
                                mxp[:, cb:cb + 1], lg[:, cb * 512:cb * 512 + cw], axis=AX.X))
                        p.em("vector", lambda e, qi=qi, nch=nchunks: e.reduce_max(
                            nmax[:, qi:qi + 1], mxp[:, 0:nch], axis=AX.X, negate=True))
                        p.em("vector", lambda e, qi=qi: e.tensor_scalar(
                            nmaxs[:, qi:qi + 1], nmax[:, qi:qi + 1], s_col, None,
                            op0=ALU.mult).then_inc(sMX, 1))
                        n_mx += 1
                        p.wait("scalar", sMX, n_mx)
                        p.em("scalar", lambda e, lg=lg, nc_=ncols, qi=qi: e.activation(
                            out=P_sb[qi][:, 0:nc_], in_=lg[:, 0:nc_], func=AF.Exp,
                            bias=nmaxs[:, qi:qi + 1], scale=s_col,
                            accum_out=srow[:, qi:qi + 1]).then_inc(sEX, 1))
                        n_ex += 1
                        if pend_norm is not None:
                            pqi, pcols, pex = pend_norm
                            p.wait("vector", sEX, pex)
                            n_rc += 1
                            p.em("vector", lambda e, qi=pqi: e.reciprocal(
                                rrow[:, qi:qi + 1], srow[:, qi:qi + 1]).then_inc(sRC, 1))
                            p.wait("vector", sRC, n_rc)
                            p.em("vector", lambda e, qi=pqi, nc_=pcols: e.tensor_scalar(
                                P_sb[qi][:, 0:nc_], P_sb[qi][:, 0:nc_], rrow[:, qi:qi + 1],
                                None, op0=ALU.mult).then_inc(sPN, 1))
                            n_pn += 1
                        pend_norm = (qi, ncols, n_ex)
                    pqi, pcols, pex = pend_norm
                    p.wait("vector", sEX, pex)
                    n_rc += 1
                    p.em("vector", lambda e, qi=pqi: e.reciprocal(
                        rrow[:, qi:qi + 1], srow[:, qi:qi + 1]).then_inc(sRC, 1))
                    p.wait("vector", sRC, n_rc)
                    p.em("vector", lambda e, qi=pqi, nc_=pcols: e.tensor_scalar(
                        P_sb[pqi][:, 0:nc_], P_sb[pqi][:, 0:nc_], rrow[:, pqi:pqi + 1],
                        None, op0=ALU.mult).then_inc(sPN, 1))
                    n_pn += 1
                    p.install_into(blk2)
                with ExitStack() as hs:
                    p = _P()
                    pv = hs.enter_context(nc.psum_tensor(f"pv{gidx}", [128, 1024], F32))
                    trB = [hs.enter_context(nc.psum_tensor(f"trB{i}_{gidx}", [128, 512], F32))
                           for i in range(2)]
                    p.wait("tensor", sEX, exbase + 7)
                    p.wait("tensor", sAE, gidx)
                    last_half_kb = {0: g * 8 + 3, 1: kbmax - 1}
                    pn_base = n_pn - 8

                    def tr_batch(srcs, dsth, dstl, dc0, width, pn_need=0):
                        nonlocal n_tb, n_txe
                        bi = n_tb % 2
                        p.wait("tensor", sTXe, n_tb - 1)
                        p.wait("tensor", sPN, pn_need)
                        for j, s_ap in enumerate(srcs):
                            p.em("tensor", lambda e, bi=bi, j=j, s_ap=s_ap: e.transpose(
                                trB[bi][:, j * 128:(j + 1) * 128], s_ap,
                                idf[:]).then_inc(sTXt, 1) if j == len(srcs) - 1 else
                                e.transpose(trB[bi][:, j * 128:(j + 1) * 128], s_ap, idf[:]))
                        n_tb += 1
                        p.wait("vector", sTXt, n_tb)
                        p.em("vector", lambda e, bi=bi, w=width, dc0=dc0, dsth=dsth: e.tensor_copy(
                            dsth[:, dc0:dc0 + w], trB[bi][:, 0:w]))
                        p.em("vector", lambda e, bi=bi, w=width, dc0=dc0, dsth=dsth, dstl=dstl: e.tensor_tensor(
                            dstl[:, dc0:dc0 + w], trB[bi][:, 0:w], dsth[:, dc0:dc0 + w],
                            op=ALU.subtract).then_inc(sTXe, 1))
                        n_txe += 1

                    nb0 = g * 8
                    for vb in range(2):
                        kb0 = nb0 + vb * 4
                        p.wait("vector", sTXt, n_tb)
                        p.em("vector", lambda e, kb0=kb0, hh=h % 2: e.tensor_tensor(
                            vf32[:], vhb[hh][:, kb0 * 128:kb0 * 128 + 512],
                            vlb[hh][:, kb0 * 128:kb0 * 128 + 512], op=ALU.add).then_inc(sVF, 1))
                        n_vf += 1
                        p.wait("tensor", sVF, n_vf)
                        tr_batch([vf32[:, j * 128:(j + 1) * 128] for j in range(4)],
                                 vth, vtl, kb0 * 128, 512)
                    def emit_pv(kb, txe_need):
                        qi_min = max(0, kb - g * 8)
                        po = (kb % 2) * 1024
                        p.wait("tensor", sTXe, txe_need)
                        for half in range(2):
                            c0 = max(qi_min * 128, half * 512)
                            c1 = (half + 1) * 512
                            if c0 >= c1:
                                continue
                            for ps in range(3):
                                lh = (vth if ps != 1 else vtl)[:, kb * 128:(kb + 1) * 128]
                                rh = (pth if ps != 2 else ptl)[:, po + c0:po + c1]
                                last = (kb == kbmax - 1 and half == 1 and ps == 2)
                                p.em("tensor", _mm(pv[:, c0:c1], lh, rh,
                                                   kb == 0 and ps == 0,
                                                   kb == last_half_kb[half] and ps == 2,
                                                   sPV if last else None))

                    prev_pv = None
                    for kb in range(kbmax):
                        qi_min = max(0, kb - g * 8)
                        po = (kb % 2) * 1024
                        qq = qi_min
                        while qq < 8:
                            nb = min(4, 8 - qq)
                            tr_batch([P_sb[qi][:, kb * 128:(kb + 1) * 128]
                                      for qi in range(qq, qq + nb)],
                                     pth, ptl, po + qq * 128, nb * 128,
                                     pn_need=pn_base + qq + nb)
                            qq += nb
                        if prev_pv is not None:
                            emit_pv(*prev_pv)
                        prev_pv = (kb, n_txe)
                    emit_pv(*prev_pv)
                    n_pv += 1
                    tb_after[gidx] = n_tb
                    p.wait("scalar", sPV, n_pv)
                    p.wait("scalar", sAO, 32 * (gidx - 1))
                    p.em("scalar", lambda e, b=gidx % 2: e.activation(
                        out=asth[b][:], in_=pv[:], func=AF.Identity, bias=0.0,
                        scale=1.0).then_inc(sAH, 1))
                    p.wait("vector", sAH, gidx + 1)
                    p.em("vector", lambda e, b=gidx % 2: e.tensor_tensor(
                        astl[b][:], pv[:], asth[b][:], op=ALU.subtract).then_inc(sAE, 1))
                    p.wait("sync", sAE, gidx + 1)
                    p.em("sync", lambda e, b=gidx % 2, h=h, g=g: e.dma_start(
                        out=bass.AP(ah, h * 128 * T + g * 1024, [[T, 128], [1, 1024]]),
                        in_=asth[b][:]).then_inc(sAO, 16))
                    p.em("sync", lambda e, b=gidx % 2, h=h, g=g: e.dma_start(
                        out=bass.AP(al, h * 128 * T + g * 1024, [[T, 128], [1, 1024]]),
                        in_=astl[b][:]).then_inc(sAO, 16))
                    p.install_into(blk2)
        p = _P()
        p.wait("sync", sAO, 32 * 32)
        p.install_into(blk2)
        blk2.__exit__(None, None, None)

    if upto < 3:
        return nc

    with ExitStack() as sc:
        athi = sc.enter_context(nc.sbuf_tensor("athi", [128, KD * 1024], BF16))
        atlo = sc.enter_context(nc.sbuf_tensor("atlo", [128, KD * 1024], BF16))
        x1pre = sc.enter_context(nc.sbuf_tensor("x1pre", [128, 8 * D], F32))
        wst3 = sc.enter_context(nc.sbuf_tensor("wst3", [128, KD * 256], F32))
        w3h = sc.enter_context(nc.sbuf_tensor("w3h", [128, KD * 256], BF16))
        w3l = sc.enter_context(nc.sbuf_tensor("w3l", [128, KD * 256], BF16))
        xres = sc.enter_context(nc.sbuf_tensor("xres", [128, 8 * 256], F32))
        bobc = sc.enter_context(nc.sbuf_tensor("bobc", [128, 256], F32))
        x1st = [sc.enter_context(nc.sbuf_tensor(f"x1st{i}", [128, D], F32)) for i in range(2)]
        x1hst = [sc.enter_context(nc.sbuf_tensor("x1hst0", [128, D], BF16))]
        x1lst = [sc.enter_context(nc.sbuf_tensor("x1lst0", [128, D], BF16))]
        stat = sc.enter_context(nc.sbuf_tensor("stat3", [128, 8], F32))
        ps3 = [sc.enter_context(nc.psum_tensor(f"ps3_{i}", [128, 256], F32)) for i in range(2)]
        bops = sc.enter_context(nc.psum_tensor("bops", [128, 256], F32))
        tr3 = [sc.enter_context(nc.psum_tensor(f"tr3_{i}", [128, 512], F32)) for i in range(2)]
        s3A = sc.enter_context(nc.semaphore("s3A"))
        s3W = sc.enter_context(nc.semaphore("s3W"))
        s3X = sc.enter_context(nc.semaphore("s3X"))
        s3SP = sc.enter_context(nc.semaphore("s3SP"))
        s3BC = sc.enter_context(nc.semaphore("s3BC"))
        s3BCE = sc.enter_context(nc.semaphore("s3BCE"))
        s3MM = sc.enter_context(nc.semaphore("s3MM"))
        s3EV = sc.enter_context(nc.semaphore("s3EV"))
        s3SQ = sc.enter_context(nc.semaphore("s3SQ"))
        s3VR = sc.enter_context(nc.semaphore("s3VR"))
        s3IV = sc.enter_context(nc.semaphore("s3IV"))
        s3LNa = sc.enter_context(nc.semaphore("s3LNa"))
        s3TRt = sc.enter_context(nc.semaphore("s3TRt"))
        s3TRe = sc.enter_context(nc.semaphore("s3TRe"))
        s3XO = sc.enter_context(nc.semaphore("s3XO"))
        s3HC = sc.enter_context(nc.semaphore("s3HC"))

        S1c, NMc, S2c, M2c, VRc, IVc, SDc = (stat[:, i:i + 1] for i in range(7))
        p = _P()
        n_tr3 = 0
        n_hc = [0]
        for grp in range(2):
            p.wait("sync", s3MM, 64 * grp)
            p.em("sync", lambda e, grp=grp: e.dma_start(
                out=athi[:], in_=_d3(ah, T, 0, KD, grp * 1024, 1024)).then_inc(s3A, 16))
            p.em("sync", lambda e, grp=grp: e.dma_start(
                out=atlo[:], in_=_d3(al, T, 0, KD, grp * 1024, 1024)).then_inc(s3A, 16))
            for wc in range(8):
                wi = grp * 8 + wc
                p.wait("sync", s3SP, wi)
                p.em("sync", lambda e, wc=wc: e.dma_start(
                    out=wst3[:], in_=_d3(wo, D, 0, KD, wc * 256, 256)).then_inc(s3W, 16))
                p.wait("sync", s3EV, 8 * wi)
                p.em("sync", lambda e, grp=grp, wc=wc: e.dma_start(
                    out=xres[:], in_=_d3(x_in, D, grp * 1024, 8, wc * 256, 256)).then_inc(s3X, 16))
                p.wait("vector", s3W, 16 * (wi + 1))
                p.wait("vector", s3MM, 8 * wi)
                p.em("vector", lambda e: e.tensor_copy(w3h[:], wst3[:]))
                p.em("vector", lambda e: e.tensor_tensor(
                    w3l[:], wst3[:], w3h[:], op=ALU.subtract).then_inc(s3SP, 1))
                p.wait("tensor", s3SP, wi + 1)
                if grp == 0 and wc == 0:
                    p.wait("tensor", s3A, 32)
                else:
                    p.wait("tensor", s3A, 32 * (grp + 1))
                p.wait("tensor", s3BCE, wi)
                p.em("tensor", lambda e, wc=wc: e.matmul(
                    bops[:], onesr[:], bor_sb[0:1, wc * 256:(wc + 1) * 256],
                    start=True, stop=True).then_inc(s3BC, 1))
                p.wait("vector", s3BC, wi + 1)
                p.em("vector", lambda e: e.tensor_copy(bobc[:], bops[:]).then_inc(s3BCE, 1))
                for tt in range(8):
                    til = wi * 8 + tt
                    p.wait("tensor", s3EV, til - 1)
                    for kc in range(16):
                        for ps in range(3):
                            lh = (athi if ps != 1 else atlo)[:, kc * 1024 + tt * 128:kc * 1024 + (tt + 1) * 128]
                            rh = (w3h if ps != 2 else w3l)[:, kc * 256:(kc + 1) * 256]
                            last = (kc == 15 and ps == 2)
                            p.em("tensor", _mm(ps3[til % 2][:], lh, rh,
                                               kc == 0 and ps == 0, last, s3MM if last else None))
                    dstc = slice(tt * D + wc * 256, tt * D + (wc + 1) * 256)
                    p.wait("vector", s3MM, til + 1)
                    p.wait("vector", s3BCE, wi + 1)
                    p.wait("vector", s3X, 16 * (wi + 1))
                    p.em("vector", lambda e, til=til, dstc=dstc: e.tensor_tensor(
                        x1pre[:, dstc], ps3[til % 2][:], bobc[:], op=ALU.add))
                    p.em("vector", lambda e, til=til, dstc=dstc, tt=tt: e.tensor_tensor(
                        x1pre[:, dstc], x1pre[:, dstc],
                        xres[:, tt * 256:(tt + 1) * 256], op=ALU.add).then_inc(s3EV, 1))
            for tt in range(8):
                ttg = grp * 8 + tt
                row = slice(tt * D, (tt + 1) * D)
                p.wait("vector", s3EV, 64 * (grp + 1))
                p.em("vector", lambda e, row=row: e.reduce_sum(S1c, x1pre[:, row], axis=AX.X))
                p.em("vector", lambda e: e.tensor_scalar(
                    NMc, S1c, nrd_col, None, op0=ALU.mult).then_inc(s3LNa, 1))
                p.wait("scalar", s3LNa, 3 * ttg + 1)
                p.wait("scalar", s3XO, 48 * (ttg - 1))
                p.em("scalar", lambda e, row=row, b=tt % 2: e.activation(
                    out=x1st[b][:], in_=x1pre[:, row], func=AF.Square,
                    accum_out=S2c).then_inc(s3SQ, 1))
                p.wait("vector", s3SQ, ttg + 1)
                p.em("vector", lambda e: e.tensor_tensor(M2c, NMc, NMc, op=ALU.mult))
                p.em("vector", lambda e: e.scalar_tensor_tensor(
                    VRc, S2c, rd_col, M2c, op0=ALU.mult, op1=ALU.subtract).then_inc(s3LNa, 1))
                p.wait("scalar", s3LNa, 3 * ttg + 2)
                p.em("scalar", lambda e: e.activation(
                    out=SDc, in_=VRc, func=AF.Sqrt, bias=eps_col, scale=1.0).then_inc(s3IV, 1))
                p.wait("vector", s3IV, ttg + 1)
                p.em("vector", lambda e: e.reciprocal(IVc, SDc).then_inc(s3VR, 1))
                p.wait("vector", s3VR, ttg + 1)
                p.em("vector", lambda e, row=row, b=tt % 2: e.tensor_scalar(
                    x1st[b][:], x1pre[:, row], NMc, IVc,
                    op0=ALU.add, op1=ALU.mult).then_inc(s3LNa, 1))
                p.wait("sync", s3LNa, 3 * ttg + 3)
                p.em("sync", lambda e, ttg=ttg, b=tt % 2: e.dma_start(
                    out=_ds(x1f, ttg * 128, 128, D), in_=x1st[b][:]).then_inc(s3XO, 16))
                p.wait("tensor", s3LNa, 3 * ttg + 3)
                for fb in range(4):
                    ti = n_tr3
                    p.wait("tensor", s3TRe, ti - 1)
                    for j in range(4):
                        f = fb * 4 + j
                        p.em("tensor", lambda e, i=ti % 2, f=f, j=j, b=tt % 2, lastj=(j == 3): (
                            e.transpose(tr3[i][:, j * 128:(j + 1) * 128],
                                        x1st[b][:, f * 128:(f + 1) * 128],
                                        idf[:]).then_inc(s3TRt, 1) if lastj else
                            e.transpose(tr3[i][:, j * 128:(j + 1) * 128],
                                        x1st[b][:, f * 128:(f + 1) * 128], idf[:])))
                    n_tr3 += 1
                    p.wait("scalar", s3TRt, n_tr3)
                    if fb == 0:
                        p.wait("scalar", s3XO, 48 * ttg)
                    p.em("scalar", lambda e, i=ti % 2, fb=fb: e.copy(
                        x1hst[0][:, fb * 512:(fb + 1) * 512], tr3[i][:]).then_inc(s3HC, 1))
                    n_hc[0] += 1
                    p.wait("vector", s3HC, n_hc[0])
                    p.em("vector", lambda e, i=ti % 2, fb=fb: e.tensor_tensor(
                        x1lst[0][:, fb * 512:(fb + 1) * 512], tr3[i][:],
                        x1hst[0][:, fb * 512:(fb + 1) * 512], op=ALU.subtract).then_inc(s3TRe, 1))
                p.wait("sync", s3TRe, n_tr3)
                p.em("sync", lambda e, ttg=ttg: e.dma_start(
                    out=_d3(x1h, T, 0, KD, ttg * 128, 128), in_=x1hst[0][:]).then_inc(s3XO, 16))
                p.em("sync", lambda e, ttg=ttg: e.dma_start(
                    out=_d3(x1l, T, 0, KD, ttg * 128, 128), in_=x1lst[0][:]).then_inc(s3XO, 16))
        p.wait("sync", s3XO, 48 * 16)
        p.install(nc)

    if upto < 4:
        return nc

    for half in range(2):
        with ExitStack() as sc:
            x1hb = sc.enter_context(nc.sbuf_tensor(f"x1hb{half}", [128, KD * 1024], BF16))
            x1lb = sc.enter_context(nc.sbuf_tensor(f"x1lb{half}", [128, KD * 1024], BF16))
            acc = sc.enter_context(nc.sbuf_tensor(f"acc{half}", [128, KD * 1024], F32))
            s4X = sc.enter_context(nc.semaphore(f"s4X{half}"))
            s4W = sc.enter_context(nc.semaphore(f"s4W{half}"))
            s4WD = sc.enter_context(nc.semaphore(f"s4WD{half}"))
            s4MG = sc.enter_context(nc.semaphore(f"s4MG{half}"))
            s4SI = sc.enter_context(nc.semaphore(f"s4SI{half}"))
            s4FU = sc.enter_context(nc.semaphore(f"s4FU{half}"))
            s4MD = sc.enter_context(nc.semaphore(f"s4MD{half}"))
            s4DA = sc.enter_context(nc.semaphore(f"s4DA{half}"))
            s4DS = sc.enter_context(nc.semaphore(f"s4DS{half}"))
            s4XF = sc.enter_context(nc.semaphore(f"s4XF{half}"))
            s4TT = sc.enter_context(nc.semaphore(f"s4TT{half}"))
            s4TE = sc.enter_context(nc.semaphore(f"s4TE{half}"))
            s4LN = sc.enter_context(nc.semaphore(f"s4LN{half}"))
            s4SQ = sc.enter_context(nc.semaphore(f"s4SQ{half}"))
            s4IV = sc.enter_context(nc.semaphore(f"s4IV{half}"))
            s4RC = sc.enter_context(nc.semaphore(f"s4RC{half}"))
            s4Y = sc.enter_context(nc.semaphore(f"s4Y{half}"))
            blk4 = nc.Block()
            blk4.__enter__()
            p = _P()
            p.em("sync", lambda e: e.dma_start(
                out=x1hb[:], in_=_d3(x1h, T, 0, KD, half * 1024, 1024)).then_inc(s4X, 16))
            p.em("sync", lambda e: e.dma_start(
                out=x1lb[:], in_=_d3(x1l, T, 0, KD, half * 1024, 1024)).then_inc(s4X, 16))
            p.em("vector", lambda e: e.memset(acc[:], 0.0))
            p.install_into(blk4)
            with ExitStack() as mmsc:
                p = _P()
                g8 = [mmsc.enter_context(nc.sbuf_tensor(f"g8_{half}{i}", [128, KD * 512], I8))
                      for i in range(2)]
                u8 = [mmsc.enter_context(nc.sbuf_tensor(f"u8_{half}{i}", [128, KD * 512], I8))
                      for i in range(2)]
                d8 = [mmsc.enter_context(nc.sbuf_tensor(f"d8_{half}0", [128, 4 * D], I8))]
                fsh = mmsc.enter_context(nc.sbuf_tensor(f"fsh{half}", [128, 4 * 1024], BF16))
                fsl = mmsc.enter_context(nc.sbuf_tensor(f"fsl{half}", [128, 4 * 1024], BF16))
                silu = [mmsc.enter_context(nc.sbuf_tensor(f"silu{half}0", [128, 512], F32))]
                fu32 = [mmsc.enter_context(nc.sbuf_tensor(f"fu32{half}0", [128, 512], F32))]
                psg = [mmsc.enter_context(nc.psum_tensor(f"psg{half}{i}", [128, 512], F32))
                       for i in range(2)]
                psu = [mmsc.enter_context(nc.psum_tensor(f"psu{half}{i}", [128, 512], F32))
                       for i in range(2)]
                psd = [mmsc.enter_context(nc.psum_tensor(f"psd{half}{i}", [128, 512], F32))
                       for i in range(2)]
                n_gu = 0
                n_dn = 0
                n_fu = 0
                n_da = 0
                for gi in range(NG):
                    p.wait("gpsimd", s4MD, 32 * (gi - 1))
                    p.em("gpsimd", lambda e, gi=gi: e.dma_start(
                        out=g8[gi % 2][:].bitcast(FP8),
                        in_=_d3(gw, NI, 0, KD, gi * 512, 512)).then_inc(s4W, 16))
                    p.em("gpsimd", lambda e, gi=gi: e.dma_start(
                        out=u8[gi % 2][:].bitcast(FP8),
                        in_=_d3(uw, NI, 0, KD, gi * 512, 512)).then_inc(s4W, 16))
                    p.wait("gpsimd", s4MD, 32 * gi)
                    p.em("gpsimd", lambda e, gi=gi: e.dma_start(
                        out=d8[0][:].bitcast(FP8),
                        in_=_d3(dw, D, gi * 512, 4, 0, D)).then_inc(s4WD, 16))
                    p.wait("tensor", s4W, 32 * (gi + 1))
                    if gi == 0:
                        p.wait("tensor", s4X, 32)
                    for iq in range(4):
                        for tc in range(2):
                            b = (iq * 2 + tc) % 2
                            step = gi * 8 + iq * 2 + tc
                            p.wait("tensor", s4FU, step - 1)
                            for mat, ps_t, sb8 in ((0, psg, g8), (1, psu, u8)):
                                for kc in range(16):
                                    for pss in range(2):
                                        lh = sb8[gi % 2][:, kc * 512 + iq * 128:kc * 512 + (iq + 1) * 128].bitcast(FP8)
                                        rh = (x1hb if pss == 0 else x1lb)[:, kc * 1024 + tc * 512:kc * 1024 + tc * 512 + 512]
                                        last = (kc == 15 and pss == 1)
                                        p.em("tensor", _mm(ps_t[b][:], lh, rh,
                                                           kc == 0 and pss == 0, last,
                                                           s4MG if last else None))
                                n_gu += 1
                            p.wait("scalar", s4MG, n_gu - 1)
                            p.wait("scalar", s4FU, n_fu)
                            gcol = gssb[:, gi * 4 + iq:gi * 4 + iq + 1]
                            p.em("scalar", lambda e, b=b, g=gcol: e.activation(
                                out=silu[0][:], in_=psg[b][:], func=AF.Silu, bias=0.0,
                                scale=g).then_inc(s4SI, 1))
                            ucol = ussb[:, gi * 4 + iq:gi * 4 + iq + 1]
                            fslc = slice(iq * 1024 + tc * 512, iq * 1024 + tc * 512 + 512)
                            p.wait("vector", s4MG, n_gu)
                            p.wait("vector", s4SI, step + 1)
                            if iq == 0 and tc == 0:
                                p.wait("vector", s4MD, 32 * gi)
                            p.em("vector", lambda e, b=b, u=ucol: e.scalar_tensor_tensor(
                                fu32[0][:], psu[b][:], u, silu[0][:], op0=ALU.mult, op1=ALU.mult))
                            p.em("vector", lambda e, fslc=fslc: e.tensor_copy(
                                fsh[:, fslc], fu32[0][:]))
                            p.em("vector", lambda e, fslc=fslc: e.tensor_tensor(
                                fsl[:, fslc], fu32[0][:], fsh[:, fslc],
                                op=ALU.subtract).then_inc(s4FU, 1))
                            n_fu += 1
                    p.wait("tensor", s4FU, n_fu)
                    p.wait("tensor", s4WD, 16 * (gi + 1))
                    for m in range(16):
                        for tc in range(2):
                            b = (m * 2 + tc) % 2
                            p.wait("tensor", s4DA, n_dn - 1)
                            for iq in range(4):
                                for pss in range(2):
                                    lh = d8[0][:, iq * D + m * 128:iq * D + (m + 1) * 128].bitcast(FP8)
                                    rh = (fsh if pss == 0 else fsl)[:, iq * 1024 + tc * 512:iq * 1024 + tc * 512 + 512]
                                    last = (iq == 3 and pss == 1)
                                    p.em("tensor", _mm(psd[b][:], lh, rh,
                                                       iq == 0 and pss == 0, last,
                                                       s4MD if last else None))
                            n_dn += 1
                            aslc = slice(m * 1024 + tc * 512, m * 1024 + tc * 512 + 512)
                            p.wait("vector", s4MD, n_dn)
                            p.em("vector", lambda e, b=b, aslc=aslc: e.tensor_tensor(
                                acc[:, aslc], acc[:, aslc], psd[b][:], op=ALU.add).then_inc(s4DA, 1))
                            n_da += 1
                p.install_into(blk4)
            blk4.__exit__(None, None, None)
            blk4b = nc.Block()
            blk4b.__enter__()
            with ExitStack() as lnsc:
                p = _P()
                ytm = lnsc.enter_context(nc.sbuf_tensor(f"ytm{half}", [128, D], F32))
                x1ft = [lnsc.enter_context(nc.sbuf_tensor(f"x1ft{half}{i}", [128, D], F32))
                        for i in range(2)]
                yst = [lnsc.enter_context(nc.sbuf_tensor(f"yst{half}{i}", [128, D], F32))
                       for i in range(2)]
                stat4 = lnsc.enter_context(nc.sbuf_tensor(f"stat4_{half}", [128, 8], F32))
                tr4 = [lnsc.enter_context(nc.psum_tensor(f"tr4_{half}{i}", [128, 128], F32))
                       for i in range(2)]
                S1c, NMc, S2c, M2c, VRc, IVc, SDc = (stat4[:, i:i + 1] for i in range(7))
                p.wait("vector", s4DA, 32 * NG)
                for m in range(16):
                    p.em("vector", lambda e, m=m: e.tensor_scalar(
                        acc[:, m * 1024:(m + 1) * 1024], acc[:, m * 1024:(m + 1) * 1024],
                        dssb[:, m:m + 1], None, op0=ALU.mult).then_inc(s4DS, 1))
                if dbg >= 2 and half == 0:
                    p.wait("sync", s4DS, 16)
                    p.em("sync", lambda e: e.dma_start(out=accdbg[:], in_=acc[:]).then_inc(sDBG, 16))
                n_t4 = 0
                for tt in range(8):
                    p.wait("sync", s4TE, 16 * (tt - 1))
                    p.em("sync", lambda e, tt=tt: e.dma_start(
                        out=x1ft[tt % 2][:],
                        in_=_ds(x1f, half * 1024 + tt * 128, 128, D)).then_inc(s4XF, 16))
                    if tt == 0:
                        p.wait("tensor", s4DS, 16)
                    for m in range(16):
                        ti = n_t4
                        p.wait("tensor", s4TE, ti - 1)
                        p.em("tensor", lambda e, i=ti % 2, m=m, tt=tt: e.transpose(
                            tr4[i][:], acc[:, m * 1024 + tt * 128:m * 1024 + (tt + 1) * 128],
                            idf[:]).then_inc(s4TT, 1))
                        p.wait("vector", s4TT, ti + 1)
                        if m == 0:
                            p.wait("vector", s4XF, 16 * (tt + 1))
                        p.em("vector", lambda e, i=ti % 2, m=m, tt=tt: e.tensor_tensor(
                            ytm[:, m * 128:(m + 1) * 128], tr4[i][:],
                            x1ft[tt % 2][:, m * 128:(m + 1) * 128], op=ALU.add).then_inc(s4TE, 1))
                        n_t4 += 1
                    if dbg >= 2 and half == 0 and tt == 0:
                        p.wait("sync", s4TE, 16)
                        p.em("sync", lambda e: e.dma_start(out=xftdbg[:], in_=x1ft[0][:]).then_inc(sDBG, 16))
                        p.em("sync", lambda e: e.dma_start(out=ytmdbg[:], in_=ytm[:]).then_inc(sDBG, 16))
                    p.em("vector", lambda e: e.reduce_sum(S1c, ytm[:], axis=AX.X))
                    p.em("vector", lambda e: e.tensor_scalar(
                        NMc, S1c, nrd_col, None, op0=ALU.mult).then_inc(s4DS, 1))
                    p.wait("scalar", s4DS, 16 + 2 * tt + 1)
                    p.wait("scalar", s4Y, 16 * (tt - 1))
                    p.em("scalar", lambda e, b=tt % 2: e.activation(
                        out=yst[b][:], in_=ytm[:], func=AF.Square, accum_out=S2c).then_inc(s4SQ, 1))
                    p.wait("vector", s4SQ, tt + 1)
                    p.em("vector", lambda e: e.tensor_tensor(M2c, NMc, NMc, op=ALU.mult))
                    p.em("vector", lambda e: e.scalar_tensor_tensor(
                        VRc, S2c, rd_col, M2c, op0=ALU.mult, op1=ALU.subtract).then_inc(s4DS, 1))
                    p.wait("scalar", s4DS, 16 + 2 * tt + 2)
                    p.em("scalar", lambda e: e.activation(
                        out=SDc, in_=VRc, func=AF.Sqrt, bias=eps_col, scale=1.0).then_inc(s4IV, 1))
                    p.wait("vector", s4IV, tt + 1)
                    p.em("vector", lambda e: e.reciprocal(IVc, SDc).then_inc(s4RC, 1))
                    p.wait("vector", s4RC, tt + 1)
                    p.em("vector", lambda e, b=tt % 2: e.tensor_scalar(
                        yst[b][:], ytm[:], NMc, IVc, op0=ALU.add, op1=ALU.mult).then_inc(s4LN, 1))
                    p.wait("sync", s4LN, tt + 1)
                    p.em("sync", lambda e, tt=tt, b=tt % 2: e.dma_start(
                        out=_ds(y_out, half * 1024 + tt * 128, 128, D),
                        in_=yst[b][:]).then_inc(s4Y, 16))
                p.wait("sync", s4Y, 16 * 8)
                p.install_into(blk4b)
            blk4b.__exit__(None, None, None)

    return nc


def _prep_inputs(kw):
    f32 = np.float32
    q = np.ascontiguousarray(np.asarray(kw["L0_attn_query"], f32))
    com = {
        "wq": np.ascontiguousarray(np.asarray(kw["L0_attn_w_q"], f32)),
        "wk": np.ascontiguousarray(np.asarray(kw["L0_attn_w_k"], f32)),
        "wv": np.ascontiguousarray(np.asarray(kw["L0_attn_w_v"], f32)),
        "wo": np.ascontiguousarray(np.asarray(kw["L0_attn_w_o"], f32)),
        "gw": np.ascontiguousarray(np.asarray(kw["L0_ffn_gate_w"], f32)),
        "uw": np.ascontiguousarray(np.asarray(kw["L0_ffn_up_w"], f32)),
        "dw": np.ascontiguousarray(np.asarray(kw["L0_ffn_down_w"], f32)),
    }
    bq = np.asarray(kw["L0_attn_b_q"], f32).reshape(KD, 128).T
    bk = np.asarray(kw["L0_attn_b_k"], f32).reshape(KD, 128).T
    bv = np.asarray(kw["L0_attn_b_v"], f32).reshape(KD, 128).T
    com["bqkv"] = np.ascontiguousarray(np.concatenate([bq, bk, bv], axis=1))
    com["bo_row"] = np.ascontiguousarray(np.asarray(kw["L0_attn_b_o"], f32).reshape(1, D))
    com["gsc"] = np.ascontiguousarray(np.asarray(kw["L0_ffn_gate_scale"], f32).reshape(KI, 128).T)
    com["usc"] = np.ascontiguousarray(np.asarray(kw["L0_ffn_up_scale"], f32).reshape(KI, 128).T)
    com["dsc"] = np.ascontiguousarray(np.asarray(kw["L0_ffn_down_scale"], f32).reshape(KD, 128).T)
    com["idf"] = np.eye(128, dtype=f32)
    com["idb"] = np.eye(128, dtype=ml_dtypes.bfloat16)
    r = np.arange(128)
    com["maskneg"] = np.where(r[:, None] >= r[None, :], 0.0, -60000.0).astype(f32)
    com["ones_row"] = np.ones((1, 128), f32)
    cv = np.zeros((128, NCONST), f32)
    cv[:, C_S] = S_ATTN
    cv[:, C_NRD] = -1.0 / D
    cv[:, C_RD] = 1.0 / D
    cv[:, C_EPS] = 1e-5
    com["cvec"] = cv
    in_maps = []
    for c in range(8):
        m = dict(com)
        m["x"] = np.ascontiguousarray(q[c])
        in_maps.append(m)
    return in_maps


def kernel(**kw):
    nc = build_nc(upto=4, dbg=False)
    in_maps = _prep_inputs(kw)
    res = run_bass_kernel_spmd(nc, in_maps, core_ids=list(range(8)))
    out = np.stack([np.asarray(r["y"], np.float32) for r in res.results], axis=0)
    return out


# version 08
# speedup vs baseline: 1.1092x; 1.1092x over previous
import numpy as np
import ml_dtypes
from contextlib import ExitStack

import concourse.bass as bass
import concourse.mybir as mybir
from concourse.bass_utils import run_bass_kernel_spmd

F32 = mybir.dt.float32
BF16 = mybir.dt.bfloat16
FP8 = mybir.dt.float8e4
I8 = mybir.dt.int8
AF = mybir.ActivationFunctionType
ALU = mybir.AluOpType
AX = mybir.AxisListType

T = 2048
D = 2048
NH = 16
HS = 128
NI = 6144
KD = D // 128
KI = NI // 128
NG = 12
S_ATTN = float(1.0 / np.sqrt(np.float64(HS)))

C_S, C_NRD, C_RD, C_EPS, NCONST = 0, 1, 2, 3, 4


def _ds(t, r0, nr, width, c0=0, ncols=None):
    if ncols is None:
        ncols = width
    return bass.AP(t, r0 * width + c0, [[width, nr], [1, ncols]])


def _d3(t, width, r0, nk, c0, ncols, kstride=128):
    return bass.AP(t, r0 * width + c0, [[width, 128], [kstride * width, nk], [1, ncols]])


class _P:

    def __init__(self):
        self.s = {k: [] for k in ("sync", "tensor", "vector", "scalar", "gpsimd")}

    def em(self, eng, fn):
        self.s[eng].append(fn)

    def wait(self, eng, sem, val):
        if val > 0:
            self.s[eng].append(lambda e, s=sem, v=val: e.wait_ge(s, v))

    def install(self, nc):
        with nc.Block() as blk:
            self.install_into(blk)

    def install_into(self, blk):
        for name, fns in self.s.items():
            if fns:
                getattr(blk, name)(lambda e, fns=fns: [f(e) for f in fns])


def _mm(ps_ap, lh, rh, st, sp, sem=None):
    if sem is None:
        return lambda e, o=ps_ap, a=lh, b=rh, st=st, sp=sp: e.matmul(o, a, b, start=st, stop=sp)
    return lambda e, o=ps_ap, a=lh, b=rh, st=st, sp=sp, s=sem: e.matmul(
        o, a, b, start=st, stop=sp).then_inc(s, 1)


def build_nc(upto=4, dbg=False):
    nc = bass.Bass("TRN2", target_bir_lowering=False, detect_race_conditions=False)

    x_in = nc.dram_tensor("x", [T, D], F32, kind="ExternalInput")
    wq = nc.dram_tensor("wq", [D, D], F32, kind="ExternalInput")
    wk = nc.dram_tensor("wk", [D, D], F32, kind="ExternalInput")
    wv = nc.dram_tensor("wv", [D, D], F32, kind="ExternalInput")
    wo = nc.dram_tensor("wo", [D, D], F32, kind="ExternalInput")
    bqkv = nc.dram_tensor("bqkv", [128, 3 * KD], F32, kind="ExternalInput")
    bo_row = nc.dram_tensor("bo_row", [1, D], F32, kind="ExternalInput")
    gw = nc.dram_tensor("gw", [D, NI], F32, kind="ExternalInput")
    uw = nc.dram_tensor("uw", [D, NI], F32, kind="ExternalInput")
    dw = nc.dram_tensor("dw", [NI, D], F32, kind="ExternalInput")
    gs_in = nc.dram_tensor("gsc", [128, KI], F32, kind="ExternalInput")
    us_in = nc.dram_tensor("usc", [128, KI], F32, kind="ExternalInput")
    ds_in = nc.dram_tensor("dsc", [128, KD], F32, kind="ExternalInput")
    idf_in = nc.dram_tensor("idf", [128, 128], F32, kind="ExternalInput")
    idb_in = nc.dram_tensor("idb", [128, 128], BF16, kind="ExternalInput")
    mask_in = nc.dram_tensor("maskneg", [128, 128], F32, kind="ExternalInput")
    ones_row_in = nc.dram_tensor("ones_row", [1, 128], F32, kind="ExternalInput")
    cvec_in = nc.dram_tensor("cvec", [128, NCONST], F32, kind="ExternalInput")

    y_out = nc.dram_tensor("y", [T, D], F32, kind="ExternalOutput")

    sk = "ExternalOutput" if dbg else "Internal"
    qh = nc.dram_tensor("qh", [D, T], BF16, kind=sk)
    ql = nc.dram_tensor("ql", [D, T], BF16, kind=sk)
    kh = nc.dram_tensor("kh", [D, T], BF16, kind=sk)
    kl = nc.dram_tensor("kl", [D, T], BF16, kind=sk)
    vh = nc.dram_tensor("vh", [D, T], BF16, kind=sk)
    vl = nc.dram_tensor("vl", [D, T], BF16, kind=sk)
    ah = nc.dram_tensor("ah", [D, T], BF16, kind=sk)
    al = nc.dram_tensor("al", [D, T], BF16, kind=sk)
    x1f = nc.dram_tensor("x1f", [T, D], F32, kind=sk)
    pdbg = nc.dram_tensor("pdbg", [128, 8 * T], F32, kind="ExternalOutput") if dbg >= 2 else None
    vtdbg = nc.dram_tensor("vtdbg", [128, 1024], BF16, kind="ExternalOutput") if dbg >= 2 else None
    stdbg = nc.dram_tensor("stdbg", [128, 32], F32, kind="ExternalOutput") if dbg >= 2 else None
    accdbg = nc.dram_tensor("accdbg", [128, 16 * 1024], F32, kind="ExternalOutput") if dbg >= 2 else None
    xftdbg = nc.dram_tensor("xftdbg", [128, D], F32, kind="ExternalOutput") if dbg >= 2 else None
    ytmdbg = nc.dram_tensor("ytmdbg", [128, D], F32, kind="ExternalOutput") if dbg >= 2 else None
    ptdbg = nc.dram_tensor("ptdbg", [128, 1024], BF16, kind="ExternalOutput") if dbg >= 2 else None
    x1h = nc.dram_tensor("x1h", [D, T], BF16, kind=sk)
    x1l = nc.dram_tensor("x1l", [D, T], BF16, kind=sk)

    ctx = nc.ctx

    idf = ctx.enter_context(nc.sbuf_tensor("c_idf", [128, 128], F32))
    idb = ctx.enter_context(nc.sbuf_tensor("c_idb", [128, 128], BF16))
    masksb = ctx.enter_context(nc.sbuf_tensor("c_mask", [128, 128], F32))
    onesr = ctx.enter_context(nc.sbuf_tensor("c_onesr", [1, 128], F32))
    cvec = ctx.enter_context(nc.sbuf_tensor("c_cvec", [128, NCONST], F32))
    bqkv_sb = ctx.enter_context(nc.sbuf_tensor("c_bqkv", [128, 3 * KD], F32))
    bor_sb = ctx.enter_context(nc.sbuf_tensor("c_bor", [1, D], F32))
    gssb = ctx.enter_context(nc.sbuf_tensor("c_gs", [128, KI], F32))
    ussb = ctx.enter_context(nc.sbuf_tensor("c_us", [128, KI], F32))
    dssb = ctx.enter_context(nc.sbuf_tensor("c_ds", [128, KD], F32))

    s_col = cvec[:, C_S:C_S + 1]
    nrd_col = cvec[:, C_NRD:C_NRD + 1]
    rd_col = cvec[:, C_RD:C_RD + 1]
    eps_col = cvec[:, C_EPS:C_EPS + 1]

    s_c = ctx.enter_context(nc.semaphore("s_c"))
    sDBG = ctx.enter_context(nc.semaphore("sDBG")) if dbg >= 2 else None
    p0 = _P()
    consts = [(idf, idf_in), (idb, idb_in), (masksb, mask_in), (onesr, ones_row_in),
              (cvec, cvec_in), (bqkv_sb, bqkv), (bor_sb, bo_row),
              (gssb, gs_in), (ussb, us_in), (dssb, ds_in)]
    for dst, src in consts:
        p0.em("sync", lambda e, d=dst, s=src: e.dma_start(out=d[:], in_=s[:]).then_inc(s_c, 16))
    p0.wait("sync", s_c, 16 * len(consts))
    p0.install(nc)

    with ExitStack() as ph1:
        xhi = ph1.enter_context(nc.sbuf_tensor("xhi", [128, KD * T], BF16))
        xlo = ph1.enter_context(nc.sbuf_tensor("xlo", [128, KD * T], BF16))
        with ExitStack() as sc:
            xtok = [sc.enter_context(nc.sbuf_tensor(f"xtok{i}", [128, D], F32)) for i in range(2)]
            wstage = [sc.enter_context(nc.sbuf_tensor(f"wst{i}", [128, KD * 128], F32)) for i in range(2)]
            whi = [sc.enter_context(nc.sbuf_tensor(f"whi{i}", [128, KD * 128], BF16)) for i in range(2)]
            wlo = [sc.enter_context(nc.sbuf_tensor(f"wlo{i}", [128, KD * 128], BF16)) for i in range(2)]
            sthi = [sc.enter_context(nc.sbuf_tensor(f"sthi{i}", [128, T], BF16)) for i in range(2)]
            stlo = [sc.enter_context(nc.sbuf_tensor(f"stlo{i}", [128, T], BF16)) for i in range(2)]
            ptr = [sc.enter_context(nc.psum_tensor(f"p1tr{i}", [128, 512], F32)) for i in range(2)]
            psB = [sc.enter_context(nc.psum_tensor(f"p1mm{i}", [128, 512], F32)) for i in range(4)]
            sA_x = sc.enter_context(nc.semaphore("sA_x"))
            sA_tr = sc.enter_context(nc.semaphore("sA_tr"))
            sA_ev = sc.enter_context(nc.semaphore("sA_ev"))
            sB_w = sc.enter_context(nc.semaphore("sB_w"))
            sB_sp = sc.enter_context(nc.semaphore("sB_sp"))
            sB_mm = sc.enter_context(nc.semaphore("sB_mm"))
            sB_eh = sc.enter_context(nc.semaphore("sB_eh"))
            sB_el = sc.enter_context(nc.semaphore("sB_el"))
            sB_o = sc.enter_context(nc.semaphore("sB_o"))
            p = _P()
            mats = [(wq, qh, ql), (wk, kh, kl), (wv, vh, vl)]

            for t in range(16):
                p.wait("sync", sA_tr, 4 * (t - 1))
                p.em("sync", lambda e, t=t: e.dma_start(
                    out=xtok[t % 2][:], in_=_ds(x_in, t * 128, 128, D)).then_inc(sA_x, 16))
            for idx in range(48):
                mat, wc = divmod(idx, 16)
                p.wait("gpsimd", sB_sp, idx - 1)
                p.em("gpsimd", lambda e, w_t=mats[mat][0], wc=wc, idx=idx: e.dma_start(
                    out=wstage[idx % 2][:], in_=_d3(w_t, D, 0, KD, wc * 128, 128)).then_inc(sB_w, 16))

            def emit_mm_group(idx, tc):
                tile = idx * 4 + tc
                p.wait("tensor", sB_sp, idx + 1)
                p.wait("tensor", sA_ev, 16 * (tc + 1) if idx == 0 else 0)
                p.wait("tensor", sB_el, tile - 3)
                for kc in range(16):
                    for ps in range(3):
                        lh = (whi if ps < 2 else wlo)[idx % 2][:, kc * 128:(kc + 1) * 128]
                        rh = (xhi if ps != 1 else xlo)[:, kc * T + tc * 512:kc * T + tc * 512 + 512]
                        last = (kc == 15 and ps == 2)
                        p.em("tensor", _mm(psB[tile % 4][:], lh, rh,
                                           kc == 0 and ps == 0, last, sB_mm if last else None))

            def emit_split(idx):
                p.wait("vector", sB_w, 16 * (idx + 1))
                p.wait("vector", sB_mm, 4 * (idx - 1))
                p.em("vector", lambda e, i=idx: e.tensor_copy(whi[i % 2][:], wstage[i % 2][:]))
                p.em("vector", lambda e, i=idx: e.tensor_tensor(
                    wlo[i % 2][:], wstage[i % 2][:], whi[i % 2][:], op=ALU.subtract).then_inc(sB_sp, 1))

            emit_split(0)
            emit_split(1)
            n_trb = 0
            for t in range(16):
                p.wait("tensor", sA_x, 16 * (t + 1))
                for fb in range(4):
                    bi = n_trb % 2
                    p.wait("tensor", sA_ev, n_trb - 1)
                    for j in range(4):
                        f = fb * 4 + j
                        p.em("tensor", lambda e, bi=bi, t=t, f=f, j=j, lastj=(j == 3): (
                            e.transpose(ptr[bi][:, j * 128:(j + 1) * 128],
                                        xtok[t % 2][:, f * 128:(f + 1) * 128],
                                        idf[:]).then_inc(sA_tr, 1) if lastj else
                            e.transpose(ptr[bi][:, j * 128:(j + 1) * 128],
                                        xtok[t % 2][:, f * 128:(f + 1) * 128], idf[:])))
                    n_trb += 1
                    p.wait("vector", sA_tr, n_trb)
                    ob = 4 * (t % 4) // 4
                    hi_ap = bass.AP(xhi, fb * 4 * T + t * 128, [[KD * T, 128], [T, 4], [1, 128]])
                    lo_ap = bass.AP(xlo, fb * 4 * T + t * 128, [[KD * T, 128], [T, 4], [1, 128]])
                    pr_ap = bass.AP(ptr[bi], 0, [[512, 128], [128, 4], [1, 128]])
                    p.em("vector", lambda e, hi_ap=hi_ap, pr_ap=pr_ap: e.tensor_copy(hi_ap, pr_ap))
                    p.em("vector", lambda e, lo_ap=lo_ap, pr_ap=pr_ap, hi_ap=hi_ap: e.tensor_tensor(
                        lo_ap, pr_ap, hi_ap, op=ALU.subtract).then_inc(sA_ev, 1))
                if t % 4 == 3:
                    emit_mm_group(0, t // 4)

            def emit_evacs(idx):
                mat, wc = divmod(idx, 16)
                b_ap = bqkv_sb[:, mat * KD + wc:mat * KD + wc + 1]
                for tc in range(4):
                    tile = idx * 4 + tc
                    p.wait("vector", sB_eh, tile + 1)
                    p.em("vector", lambda e, i=idx, tc=tc, tile=tile, b=b_ap: e.scalar_tensor_tensor(
                        stlo[i % 2][:, tc * 512:(tc + 1) * 512], psB[tile % 4][:], b,
                        sthi[i % 2][:, tc * 512:(tc + 1) * 512],
                        op0=ALU.add, op1=ALU.subtract).then_inc(sB_el, 1))

            for idx in range(2, 48):
                emit_split(idx)
                emit_evacs(idx - 2)
            emit_evacs(46)
            emit_evacs(47)

            for idx in range(1, 48):
                for tc in range(4):
                    emit_mm_group(idx, tc)

            for tile in range(192):
                idx, tc = divmod(tile, 4)
                mat, wc = divmod(idx, 16)
                b_ap = bqkv_sb[:, mat * KD + wc:mat * KD + wc + 1]
                p.wait("scalar", sB_mm, tile + 1)
                if tc == 0:
                    p.wait("scalar", sB_o, 32 * (idx - 1))
                p.em("scalar", lambda e, i=idx, tc=tc, tile=tile, b=b_ap: e.activation(
                    out=sthi[i % 2][:, tc * 512:(tc + 1) * 512], in_=psB[tile % 4][:],
                    func=AF.Identity, bias=b, scale=1.0).then_inc(sB_eh, 1))

            for idx in range(48):
                mat, wc = divmod(idx, 16)
                p.wait("sync", sB_el, 4 * (idx + 1))
                p.em("sync", lambda e, i=idx, t=mats[mat][1], wc=wc: e.dma_start(
                    out=_ds(t, wc * 128, 128, T), in_=sthi[i % 2][:]).then_inc(sB_o, 16))
                p.em("sync", lambda e, i=idx, t=mats[mat][2], wc=wc: e.dma_start(
                    out=_ds(t, wc * 128, 128, T), in_=stlo[i % 2][:]).then_inc(sB_o, 16))
            p.wait("sync", sB_o, 16 * 96)
            p.install(nc)

    if upto < 2:
        return nc

    with ExitStack() as sc:
        qhb = [sc.enter_context(nc.sbuf_tensor(f"qhb{i}", [128, T], BF16)) for i in range(2)]
        qlb = [sc.enter_context(nc.sbuf_tensor(f"qlb{i}", [128, T], BF16)) for i in range(2)]
        khb = [sc.enter_context(nc.sbuf_tensor(f"khb{i}", [128, T], BF16)) for i in range(2)]
        klb = [sc.enter_context(nc.sbuf_tensor(f"klb{i}", [128, T], BF16)) for i in range(2)]
        vhb = [sc.enter_context(nc.sbuf_tensor(f"vhb{i}", [128, T], BF16)) for i in range(2)]
        vlb = [sc.enter_context(nc.sbuf_tensor(f"vlb{i}", [128, T], BF16)) for i in range(2)]
        P_sb = [sc.enter_context(nc.sbuf_tensor(f"Psb{i}", [128, T], F32)) for i in range(8)]
        vth = sc.enter_context(nc.sbuf_tensor("vth", [128, T], BF16))
        vtl = sc.enter_context(nc.sbuf_tensor("vtl", [128, T], BF16))
        vf32 = sc.enter_context(nc.sbuf_tensor("vf32", [128, 512], F32))
        pth = sc.enter_context(nc.sbuf_tensor("pth", [128, 2048], BF16))
        ptl = sc.enter_context(nc.sbuf_tensor("ptl", [128, 2048], BF16))
        asth = [sc.enter_context(nc.sbuf_tensor(f"asth{i}", [128, 1024], BF16)) for i in range(2)]
        astl = [sc.enter_context(nc.sbuf_tensor(f"astl{i}", [128, 1024], BF16)) for i in range(2)]
        nmax = sc.enter_context(nc.sbuf_tensor("nmax", [128, 8], F32))
        nmaxs = sc.enter_context(nc.sbuf_tensor("nmaxs", [128, 8], F32))
        srow = sc.enter_context(nc.sbuf_tensor("srow", [128, 8], F32))
        rrow = sc.enter_context(nc.sbuf_tensor("rrow", [128, 8], F32))
        mxp = sc.enter_context(nc.sbuf_tensor("mxp", [128, 4], F32))
        sH = sc.enter_context(nc.semaphore("sH"))
        sLG = sc.enter_context(nc.semaphore("sLG"))
        sMX = sc.enter_context(nc.semaphore("sMX"))
        sEX = sc.enter_context(nc.semaphore("sEX"))
        sPN = sc.enter_context(nc.semaphore("sPN"))
        sVF = sc.enter_context(nc.semaphore("sVF"))
        sTXt = sc.enter_context(nc.semaphore("sTXt"))
        sTH = sc.enter_context(nc.semaphore("sTH"))
        sTXe = sc.enter_context(nc.semaphore("sTXe"))
        sPV = sc.enter_context(nc.semaphore("sPV"))
        sAH = sc.enter_context(nc.semaphore("sAH"))
        sAE = sc.enter_context(nc.semaphore("sAE"))
        sAO = sc.enter_context(nc.semaphore("sAO"))
        sRC = sc.enter_context(nc.semaphore("sRC"))

        n_h = 0
        n_lgc = n_mx = n_ex = n_pn = n_rc = 0
        n_vf = 0
        n_tb = 0
        n_th = 0
        n_txe = 0
        n_pv = 0
        tb_after = {}

        blk2 = nc.Block()
        blk2.__enter__()
        for h in range(16):
            p = _P()
            p.wait("sync", sAE, 2 * (h - 1))
            for dst, src2 in [(qhb, qh), (qlb, ql), (khb, kh), (klb, kl), (vhb, vh), (vlb, vl)]:
                p.em("sync", lambda e, d=dst[h % 2], s=src2, h=h: e.dma_start(
                    out=d[:], in_=_ds(s, h * 128, 128, T)).then_inc(sH, 16))
            n_h += 6
            p.install_into(blk2)

            for g in range(2):
                gidx = 2 * h + g
                kbmax = g * 8 + 8
                with ExitStack() as hs:
                    p = _P()
                    lgA = hs.enter_context(nc.psum_tensor(f"lgA{gidx}", [128, T], F32))
                    lgB = hs.enter_context(nc.psum_tensor(f"lgB{gidx}", [128, T], F32))
                    exbase = n_ex
                    p.wait("tensor", sH, 16 * n_h)
                    p.wait("tensor", sAE, gidx)
                    if gidx >= 1:
                        p.wait("scalar", sTXt, tb_after[gidx - 1])
                    pend_norm = None
                    for qi in range(8):
                        qg = g * 8 + qi
                        ncols = (qg + 1) * 128
                        lg = lgA if qi % 2 == 0 else lgB
                        nchunks = (ncols + 511) // 512
                        p.wait("tensor", sEX, exbase + qi - 1)
                        ch_base = n_lgc
                        for cb in range(nchunks):
                            cw = min(512, ncols - cb * 512)
                            for ps in range(3):
                                lh = (qhb if ps != 1 else qlb)[h % 2][:, qg * 128:(qg + 1) * 128]
                                rh = (khb if ps != 2 else klb)[h % 2][:, cb * 512:cb * 512 + cw]
                                p.em("tensor", _mm(lg[:, cb * 512:cb * 512 + cw], lh, rh,
                                                   ps == 0, ps == 2, sLG if ps == 2 else None))
                            n_lgc += 1
                        for cb in range(nchunks):
                            cw = min(512, ncols - cb * 512)
                            p.wait("vector", sLG, ch_base + cb + 1)
                            if cb == nchunks - 1:
                                p.em("vector", lambda e, lg=lg, nc_=ncols: e.tensor_tensor(
                                    lg[:, nc_ - 128:nc_], lg[:, nc_ - 128:nc_], masksb[:], op=ALU.add))
                            p.em("vector", lambda e, lg=lg, cb=cb, cw=cw: e.reduce_max(
                                mxp[:, cb:cb + 1], lg[:, cb * 512:cb * 512 + cw], axis=AX.X))
                        p.em("vector", lambda e, qi=qi, nch=nchunks: e.reduce_max(
                            nmax[:, qi:qi + 1], mxp[:, 0:nch], axis=AX.X, negate=True))
                        p.em("vector", lambda e, qi=qi: e.tensor_scalar(
                            nmaxs[:, qi:qi + 1], nmax[:, qi:qi + 1], s_col, None,
                            op0=ALU.mult).then_inc(sMX, 1))
                        n_mx += 1
                        p.wait("scalar", sMX, n_mx)
                        p.em("scalar", lambda e, lg=lg, nc_=ncols, qi=qi: e.activation(
                            out=P_sb[qi][:, 0:nc_], in_=lg[:, 0:nc_], func=AF.Exp,
                            bias=nmaxs[:, qi:qi + 1], scale=s_col,
                            accum_out=srow[:, qi:qi + 1]).then_inc(sEX, 1))
                        n_ex += 1
                        if pend_norm is not None:
                            pqi, pcols, pex = pend_norm
                            p.wait("vector", sEX, pex)
                            n_rc += 1
                            p.em("vector", lambda e, qi=pqi: e.reciprocal(
                                rrow[:, qi:qi + 1], srow[:, qi:qi + 1]).then_inc(sRC, 1))
                            p.wait("vector", sRC, n_rc)
                            p.em("vector", lambda e, qi=pqi, nc_=pcols: e.tensor_scalar(
                                P_sb[qi][:, 0:nc_], P_sb[qi][:, 0:nc_], rrow[:, qi:qi + 1],
                                None, op0=ALU.mult).then_inc(sPN, 1))
                            n_pn += 1
                        pend_norm = (qi, ncols, n_ex)
                    pqi, pcols, pex = pend_norm
                    p.wait("vector", sEX, pex)
                    n_rc += 1
                    p.em("vector", lambda e, qi=pqi: e.reciprocal(
                        rrow[:, qi:qi + 1], srow[:, qi:qi + 1]).then_inc(sRC, 1))
                    p.wait("vector", sRC, n_rc)
                    p.em("vector", lambda e, qi=pqi, nc_=pcols: e.tensor_scalar(
                        P_sb[pqi][:, 0:nc_], P_sb[pqi][:, 0:nc_], rrow[:, pqi:pqi + 1],
                        None, op0=ALU.mult).then_inc(sPN, 1))
                    n_pn += 1
                    p.install_into(blk2)
                with ExitStack() as hs:
                    p = _P()
                    pv = hs.enter_context(nc.psum_tensor(f"pv{gidx}", [128, 1024], F32))
                    trB = [hs.enter_context(nc.psum_tensor(f"trB{i}_{gidx}", [128, 512], F32))
                           for i in range(2)]
                    p.wait("tensor", sEX, exbase + 7)
                    p.wait("tensor", sAE, gidx)
                    last_half_kb = {0: g * 8 + 3, 1: kbmax - 1}
                    pn_base = n_pn - 8

                    def tr_batch(srcs, dsth, dstl, dc0, width, pn_need=0):
                        nonlocal n_tb, n_txe
                        bi = n_tb % 2
                        p.wait("tensor", sTXe, n_tb - 1)
                        p.wait("tensor", sPN, pn_need)
                        for j, s_ap in enumerate(srcs):
                            p.em("tensor", lambda e, bi=bi, j=j, s_ap=s_ap: e.transpose(
                                trB[bi][:, j * 128:(j + 1) * 128], s_ap,
                                idf[:]).then_inc(sTXt, 1) if j == len(srcs) - 1 else
                                e.transpose(trB[bi][:, j * 128:(j + 1) * 128], s_ap, idf[:]))
                        n_tb += 1
                        p.wait("vector", sTXt, n_tb)
                        p.em("vector", lambda e, bi=bi, w=width, dc0=dc0, dsth=dsth: e.tensor_copy(
                            dsth[:, dc0:dc0 + w], trB[bi][:, 0:w]))
                        p.em("vector", lambda e, bi=bi, w=width, dc0=dc0, dsth=dsth, dstl=dstl: e.tensor_tensor(
                            dstl[:, dc0:dc0 + w], trB[bi][:, 0:w], dsth[:, dc0:dc0 + w],
                            op=ALU.subtract).then_inc(sTXe, 1))
                        n_txe += 1

                    nb0 = g * 8
                    for vb in range(2):
                        kb0 = nb0 + vb * 4
                        p.wait("vector", sTXt, n_tb)
                        p.em("vector", lambda e, kb0=kb0, hh=h % 2: e.tensor_tensor(
                            vf32[:], vhb[hh][:, kb0 * 128:kb0 * 128 + 512],
                            vlb[hh][:, kb0 * 128:kb0 * 128 + 512], op=ALU.add).then_inc(sVF, 1))
                        n_vf += 1
                        p.wait("tensor", sVF, n_vf)
                        tr_batch([vf32[:, j * 128:(j + 1) * 128] for j in range(4)],
                                 vth, vtl, kb0 * 128, 512)
                    def emit_pv(kb, txe_need):
                        qi_min = max(0, kb - g * 8)
                        po = (kb % 2) * 1024
                        p.wait("tensor", sTXe, txe_need)
                        for half in range(2):
                            c0 = max(qi_min * 128, half * 512)
                            c1 = (half + 1) * 512
                            if c0 >= c1:
                                continue
                            for ps in range(3):
                                lh = (vth if ps != 1 else vtl)[:, kb * 128:(kb + 1) * 128]
                                rh = (pth if ps != 2 else ptl)[:, po + c0:po + c1]
                                last = (kb == kbmax - 1 and half == 1 and ps == 2)
                                p.em("tensor", _mm(pv[:, c0:c1], lh, rh,
                                                   kb == 0 and ps == 0,
                                                   kb == last_half_kb[half] and ps == 2,
                                                   sPV if last else None))

                    prev_pv = None
                    for kb in range(kbmax):
                        qi_min = max(0, kb - g * 8)
                        po = (kb % 2) * 1024
                        qq = qi_min
                        while qq < 8:
                            nb = min(4, 8 - qq)
                            tr_batch([P_sb[qi][:, kb * 128:(kb + 1) * 128]
                                      for qi in range(qq, qq + nb)],
                                     pth, ptl, po + qq * 128, nb * 128,
                                     pn_need=pn_base + qq + nb)
                            qq += nb
                        if prev_pv is not None:
                            emit_pv(*prev_pv)
                        prev_pv = (kb, n_txe)
                    emit_pv(*prev_pv)
                    n_pv += 1
                    tb_after[gidx] = n_tb
                    p.wait("scalar", sPV, n_pv)
                    p.wait("scalar", sAO, 32 * (gidx - 1))
                    p.em("scalar", lambda e, b=gidx % 2: e.activation(
                        out=asth[b][:], in_=pv[:], func=AF.Identity, bias=0.0,
                        scale=1.0).then_inc(sAH, 1))
                    p.wait("vector", sAH, gidx + 1)
                    p.em("vector", lambda e, b=gidx % 2: e.tensor_tensor(
                        astl[b][:], pv[:], asth[b][:], op=ALU.subtract).then_inc(sAE, 1))
                    p.wait("sync", sAE, gidx + 1)
                    p.em("sync", lambda e, b=gidx % 2, h=h, g=g: e.dma_start(
                        out=bass.AP(ah, h * 128 * T + g * 1024, [[T, 128], [1, 1024]]),
                        in_=asth[b][:]).then_inc(sAO, 16))
                    p.em("sync", lambda e, b=gidx % 2, h=h, g=g: e.dma_start(
                        out=bass.AP(al, h * 128 * T + g * 1024, [[T, 128], [1, 1024]]),
                        in_=astl[b][:]).then_inc(sAO, 16))
                    p.install_into(blk2)
        p = _P()
        p.wait("sync", sAO, 32 * 32)
        p.install_into(blk2)
        blk2.__exit__(None, None, None)

    if upto < 3:
        return nc

    with ExitStack() as sc:
        athi = sc.enter_context(nc.sbuf_tensor("athi", [128, KD * 1024], BF16))
        atlo = sc.enter_context(nc.sbuf_tensor("atlo", [128, KD * 1024], BF16))
        x1pre = sc.enter_context(nc.sbuf_tensor("x1pre", [128, 8 * D], F32))
        wst3 = sc.enter_context(nc.sbuf_tensor("wst3", [128, KD * 256], F32))
        w3h = sc.enter_context(nc.sbuf_tensor("w3h", [128, KD * 256], BF16))
        w3l = sc.enter_context(nc.sbuf_tensor("w3l", [128, KD * 256], BF16))
        xres = sc.enter_context(nc.sbuf_tensor("xres", [128, 8 * 256], F32))
        bobc = sc.enter_context(nc.sbuf_tensor("bobc", [128, D], F32))
        x1st = [sc.enter_context(nc.sbuf_tensor("x1st0", [128, D], F32))]
        x1hst = [sc.enter_context(nc.sbuf_tensor("x1hst0", [128, D], BF16))]
        x1lst = [sc.enter_context(nc.sbuf_tensor("x1lst0", [128, D], BF16))]
        stat = sc.enter_context(nc.sbuf_tensor("stat3", [128, 8], F32))
        ps3 = [sc.enter_context(nc.psum_tensor(f"ps3_{i}", [128, 256], F32)) for i in range(4)]
        bops = sc.enter_context(nc.psum_tensor("bops", [128, 512], F32))
        tr3 = [sc.enter_context(nc.psum_tensor(f"tr3_{i}", [128, 512], F32)) for i in range(2)]
        s3A = sc.enter_context(nc.semaphore("s3A"))
        s3W = sc.enter_context(nc.semaphore("s3W"))
        s3X = sc.enter_context(nc.semaphore("s3X"))
        s3SP = sc.enter_context(nc.semaphore("s3SP"))
        s3BC = sc.enter_context(nc.semaphore("s3BC"))
        s3BCE = sc.enter_context(nc.semaphore("s3BCE"))
        s3MM = sc.enter_context(nc.semaphore("s3MM"))
        s3EV = sc.enter_context(nc.semaphore("s3EV"))
        s3SQ = sc.enter_context(nc.semaphore("s3SQ"))
        s3VR = sc.enter_context(nc.semaphore("s3VR"))
        s3IV = sc.enter_context(nc.semaphore("s3IV"))
        s3LNa = sc.enter_context(nc.semaphore("s3LNa"))
        s3TRt = sc.enter_context(nc.semaphore("s3TRt"))
        s3TRe = sc.enter_context(nc.semaphore("s3TRe"))
        s3XO = sc.enter_context(nc.semaphore("s3XO"))
        s3HC = sc.enter_context(nc.semaphore("s3HC"))

        S1c, NMc, S2c, M2c, VRc, IVc, SDc = (stat[:, i:i + 1] for i in range(7))
        p = _P()
        n_tr3 = 0
        n_hc = [0]
        for bcw in range(4):
            p.wait("tensor", s3BC, 2 * bcw)
            p.em("tensor", lambda e, bcw=bcw: e.matmul(
                bops[:, 0:512], onesr[:], bor_sb[0:1, bcw * 512:(bcw + 1) * 512],
                start=True, stop=True).then_inc(s3BC, 1))
            p.wait("vector", s3BC, 2 * bcw + 1)
            p.em("vector", lambda e, bcw=bcw: e.tensor_copy(
                bobc[:, bcw * 512:(bcw + 1) * 512], bops[:, 0:512]).then_inc(s3BC, 1))
        for grp in range(2):
            p.wait("sync", s3MM, 64 * grp)
            p.em("sync", lambda e, grp=grp: e.dma_start(
                out=athi[:], in_=_d3(ah, T, 0, KD, grp * 1024, 1024)).then_inc(s3A, 16))
            p.em("sync", lambda e, grp=grp: e.dma_start(
                out=atlo[:], in_=_d3(al, T, 0, KD, grp * 1024, 1024)).then_inc(s3A, 16))
            for wc in range(8):
                wi = grp * 8 + wc
                p.wait("sync", s3SP, wi)
                p.em("sync", lambda e, wc=wc: e.dma_start(
                    out=wst3[:], in_=_d3(wo, D, 0, KD, wc * 256, 256)).then_inc(s3W, 16))
                p.wait("sync", s3EV, 8 * wi)
                p.em("sync", lambda e, grp=grp, wc=wc: e.dma_start(
                    out=xres[:], in_=_d3(x_in, D, grp * 1024, 8, wc * 256, 256)).then_inc(s3X, 16))
                p.wait("vector", s3W, 16 * (wi + 1))
                p.wait("vector", s3MM, 8 * wi)
                p.em("vector", lambda e: e.tensor_copy(w3h[:], wst3[:]))
                p.em("vector", lambda e: e.tensor_tensor(
                    w3l[:], wst3[:], w3h[:], op=ALU.subtract).then_inc(s3SP, 1))
                p.wait("tensor", s3SP, wi + 1)
                p.wait("tensor", s3A, 32 * (grp + 1))
                for tt in range(8):
                    til = wi * 8 + tt
                    p.wait("tensor", s3EV, til - 3)
                    for kc in range(16):
                        for ps in range(3):
                            lh = (athi if ps != 1 else atlo)[:, kc * 1024 + tt * 128:kc * 1024 + (tt + 1) * 128]
                            rh = (w3h if ps != 2 else w3l)[:, kc * 256:(kc + 1) * 256]
                            last = (kc == 15 and ps == 2)
                            p.em("tensor", _mm(ps3[til % 4][:], lh, rh,
                                               kc == 0 and ps == 0, last, s3MM if last else None))
                    dstc = slice(tt * D + wc * 256, tt * D + (wc + 1) * 256)
                    p.wait("vector", s3MM, til + 1)
                    p.wait("vector", s3X, 16 * (wi + 1))
                    p.em("vector", lambda e, til=til, dstc=dstc, wc=wc: e.tensor_tensor(
                        x1pre[:, dstc], ps3[til % 4][:],
                        bobc[:, wc * 256:(wc + 1) * 256], op=ALU.add))
                    p.em("vector", lambda e, til=til, dstc=dstc, tt=tt: e.tensor_tensor(
                        x1pre[:, dstc], x1pre[:, dstc],
                        xres[:, tt * 256:(tt + 1) * 256], op=ALU.add).then_inc(s3EV, 1))
            for tt in range(8):
                ttg = grp * 8 + tt
                row = slice(tt * D, (tt + 1) * D)
                p.wait("vector", s3EV, 64 * (grp + 1))
                p.em("vector", lambda e, row=row: e.reduce_sum(S1c, x1pre[:, row], axis=AX.X))
                p.em("vector", lambda e: e.tensor_scalar(
                    NMc, S1c, nrd_col, None, op0=ALU.mult).then_inc(s3LNa, 1))
                p.wait("scalar", s3LNa, 3 * ttg + 1)
                p.wait("scalar", s3XO, 48 * ttg)
                p.em("scalar", lambda e, row=row: e.activation(
                    out=x1st[0][:], in_=x1pre[:, row], func=AF.Square,
                    accum_out=S2c).then_inc(s3SQ, 1))
                p.wait("vector", s3SQ, ttg + 1)
                p.em("vector", lambda e: e.tensor_tensor(M2c, NMc, NMc, op=ALU.mult))
                p.em("vector", lambda e: e.scalar_tensor_tensor(
                    VRc, S2c, rd_col, M2c, op0=ALU.mult, op1=ALU.subtract).then_inc(s3LNa, 1))
                p.wait("scalar", s3LNa, 3 * ttg + 2)
                p.em("scalar", lambda e: e.activation(
                    out=SDc, in_=VRc, func=AF.Sqrt, bias=eps_col, scale=1.0).then_inc(s3IV, 1))
                p.wait("vector", s3IV, ttg + 1)
                p.em("vector", lambda e: e.reciprocal(IVc, SDc).then_inc(s3VR, 1))
                p.wait("vector", s3VR, ttg + 1)
                p.em("vector", lambda e, row=row: e.tensor_scalar(
                    x1st[0][:], x1pre[:, row], NMc, IVc,
                    op0=ALU.add, op1=ALU.mult).then_inc(s3LNa, 1))
                p.wait("sync", s3LNa, 3 * ttg + 3)
                p.em("sync", lambda e, ttg=ttg: e.dma_start(
                    out=_ds(x1f, ttg * 128, 128, D), in_=x1st[0][:]).then_inc(s3XO, 16))
                p.wait("tensor", s3LNa, 3 * ttg + 3)
                for fb in range(4):
                    ti = n_tr3
                    p.wait("tensor", s3TRe, ti - 1)
                    for j in range(4):
                        f = fb * 4 + j
                        p.em("tensor", lambda e, i=ti % 2, f=f, j=j, lastj=(j == 3): (
                            e.transpose(tr3[i][:, j * 128:(j + 1) * 128],
                                        x1st[0][:, f * 128:(f + 1) * 128],
                                        idf[:]).then_inc(s3TRt, 1) if lastj else
                            e.transpose(tr3[i][:, j * 128:(j + 1) * 128],
                                        x1st[0][:, f * 128:(f + 1) * 128], idf[:])))
                    n_tr3 += 1
                    p.wait("scalar", s3TRt, n_tr3)
                    if fb == 0:
                        p.wait("scalar", s3XO, 48 * ttg)
                    p.em("scalar", lambda e, i=ti % 2, fb=fb: e.copy(
                        x1hst[0][:, fb * 512:(fb + 1) * 512], tr3[i][:]).then_inc(s3HC, 1))
                    n_hc[0] += 1
                    p.wait("vector", s3HC, n_hc[0])
                    p.em("vector", lambda e, i=ti % 2, fb=fb: e.tensor_tensor(
                        x1lst[0][:, fb * 512:(fb + 1) * 512], tr3[i][:],
                        x1hst[0][:, fb * 512:(fb + 1) * 512], op=ALU.subtract).then_inc(s3TRe, 1))
                p.wait("sync", s3TRe, n_tr3)
                p.em("sync", lambda e, ttg=ttg: e.dma_start(
                    out=_d3(x1h, T, 0, KD, ttg * 128, 128), in_=x1hst[0][:]).then_inc(s3XO, 16))
                p.em("sync", lambda e, ttg=ttg: e.dma_start(
                    out=_d3(x1l, T, 0, KD, ttg * 128, 128), in_=x1lst[0][:]).then_inc(s3XO, 16))
        p.wait("sync", s3XO, 48 * 16)
        p.install(nc)

    if upto < 4:
        return nc

    for half in range(2):
        with ExitStack() as sc:
            x1hb = sc.enter_context(nc.sbuf_tensor(f"x1hb{half}", [128, KD * 1024], BF16))
            x1lb = sc.enter_context(nc.sbuf_tensor(f"x1lb{half}", [128, KD * 1024], BF16))
            acc = sc.enter_context(nc.sbuf_tensor(f"acc{half}", [128, KD * 1024], F32))
            s4X = sc.enter_context(nc.semaphore(f"s4X{half}"))
            s4W = sc.enter_context(nc.semaphore(f"s4W{half}"))
            s4WD = sc.enter_context(nc.semaphore(f"s4WD{half}"))
            s4MG = sc.enter_context(nc.semaphore(f"s4MG{half}"))
            s4SI = sc.enter_context(nc.semaphore(f"s4SI{half}"))
            s4FU = sc.enter_context(nc.semaphore(f"s4FU{half}"))
            s4MD = sc.enter_context(nc.semaphore(f"s4MD{half}"))
            s4DA = sc.enter_context(nc.semaphore(f"s4DA{half}"))
            s4DS = sc.enter_context(nc.semaphore(f"s4DS{half}"))
            s4XF = sc.enter_context(nc.semaphore(f"s4XF{half}"))
            s4TT = sc.enter_context(nc.semaphore(f"s4TT{half}"))
            s4TE = sc.enter_context(nc.semaphore(f"s4TE{half}"))
            s4LN = sc.enter_context(nc.semaphore(f"s4LN{half}"))
            s4SQ = sc.enter_context(nc.semaphore(f"s4SQ{half}"))
            s4IV = sc.enter_context(nc.semaphore(f"s4IV{half}"))
            s4RC = sc.enter_context(nc.semaphore(f"s4RC{half}"))
            s4Y = sc.enter_context(nc.semaphore(f"s4Y{half}"))
            blk4 = nc.Block()
            blk4.__enter__()
            p = _P()
            p.em("sync", lambda e: e.dma_start(
                out=x1hb[:], in_=_d3(x1h, T, 0, KD, half * 1024, 1024)).then_inc(s4X, 16))
            p.em("sync", lambda e: e.dma_start(
                out=x1lb[:], in_=_d3(x1l, T, 0, KD, half * 1024, 1024)).then_inc(s4X, 16))
            p.em("vector", lambda e: e.memset(acc[:], 0.0))
            p.install_into(blk4)
            with ExitStack() as mmsc:
                p = _P()
                g8 = [mmsc.enter_context(nc.sbuf_tensor(f"g8_{half}{i}", [128, KD * 512], I8))
                      for i in range(2)]
                u8 = [mmsc.enter_context(nc.sbuf_tensor(f"u8_{half}{i}", [128, KD * 512], I8))
                      for i in range(2)]
                d8 = [mmsc.enter_context(nc.sbuf_tensor(f"d8_{half}0", [128, 4 * D], I8))]
                fsh = mmsc.enter_context(nc.sbuf_tensor(f"fsh{half}", [128, 4 * 1024], BF16))
                fsl = mmsc.enter_context(nc.sbuf_tensor(f"fsl{half}", [128, 4 * 1024], BF16))
                silu = [mmsc.enter_context(nc.sbuf_tensor(f"silu{half}0", [128, 512], F32))]
                fu32 = [mmsc.enter_context(nc.sbuf_tensor(f"fu32{half}0", [128, 512], F32))]
                psg = [mmsc.enter_context(nc.psum_tensor(f"psg{half}{i}", [128, 512], F32))
                       for i in range(2)]
                psu = [mmsc.enter_context(nc.psum_tensor(f"psu{half}{i}", [128, 512], F32))
                       for i in range(2)]
                psd = [mmsc.enter_context(nc.psum_tensor(f"psd{half}{i}", [128, 512], F32))
                       for i in range(2)]
                n_gu = 0
                n_dn = 0
                n_fu = 0
                n_da = 0
                for gi in range(NG):
                    p.wait("gpsimd", s4MD, 32 * (gi - 1))
                    p.em("gpsimd", lambda e, gi=gi: e.dma_start(
                        out=g8[gi % 2][:].bitcast(FP8),
                        in_=_d3(gw, NI, 0, KD, gi * 512, 512)).then_inc(s4W, 16))
                    p.em("gpsimd", lambda e, gi=gi: e.dma_start(
                        out=u8[gi % 2][:].bitcast(FP8),
                        in_=_d3(uw, NI, 0, KD, gi * 512, 512)).then_inc(s4W, 16))
                    p.wait("gpsimd", s4MD, 32 * gi)
                    p.em("gpsimd", lambda e, gi=gi: e.dma_start(
                        out=d8[0][:].bitcast(FP8),
                        in_=_d3(dw, D, gi * 512, 4, 0, D)).then_inc(s4WD, 16))
                    p.wait("tensor", s4W, 32 * (gi + 1))
                    if gi == 0:
                        p.wait("tensor", s4X, 32)
                    for iq in range(4):
                        for tc in range(2):
                            b = (iq * 2 + tc) % 2
                            step = gi * 8 + iq * 2 + tc
                            p.wait("tensor", s4FU, step - 1)
                            for mat, ps_t, sb8 in ((0, psg, g8), (1, psu, u8)):
                                for kc in range(16):
                                    for pss in range(2):
                                        lh = sb8[gi % 2][:, kc * 512 + iq * 128:kc * 512 + (iq + 1) * 128].bitcast(FP8)
                                        rh = (x1hb if pss == 0 else x1lb)[:, kc * 1024 + tc * 512:kc * 1024 + tc * 512 + 512]
                                        last = (kc == 15 and pss == 1)
                                        p.em("tensor", _mm(ps_t[b][:], lh, rh,
                                                           kc == 0 and pss == 0, last,
                                                           s4MG if last else None))
                                n_gu += 1
                            p.wait("scalar", s4MG, n_gu - 1)
                            p.wait("scalar", s4FU, n_fu)
                            gcol = gssb[:, gi * 4 + iq:gi * 4 + iq + 1]
                            p.em("scalar", lambda e, b=b, g=gcol: e.activation(
                                out=silu[0][:], in_=psg[b][:], func=AF.Silu, bias=0.0,
                                scale=g).then_inc(s4SI, 1))
                            ucol = ussb[:, gi * 4 + iq:gi * 4 + iq + 1]
                            fslc = slice(iq * 1024 + tc * 512, iq * 1024 + tc * 512 + 512)
                            p.wait("vector", s4MG, n_gu)
                            p.wait("vector", s4SI, step + 1)
                            if iq == 0 and tc == 0:
                                p.wait("vector", s4MD, 32 * gi)
                            p.em("vector", lambda e, b=b, u=ucol: e.scalar_tensor_tensor(
                                fu32[0][:], psu[b][:], u, silu[0][:], op0=ALU.mult, op1=ALU.mult))
                            p.em("vector", lambda e, fslc=fslc: e.tensor_copy(
                                fsh[:, fslc], fu32[0][:]))
                            p.em("vector", lambda e, fslc=fslc: e.tensor_tensor(
                                fsl[:, fslc], fu32[0][:], fsh[:, fslc],
                                op=ALU.subtract).then_inc(s4FU, 1))
                            n_fu += 1
                    p.wait("tensor", s4FU, n_fu)
                    p.wait("tensor", s4WD, 16 * (gi + 1))
                    for m in range(16):
                        for tc in range(2):
                            b = (m * 2 + tc) % 2
                            p.wait("tensor", s4DA, n_dn - 1)
                            for iq in range(4):
                                for pss in range(2):
                                    lh = d8[0][:, iq * D + m * 128:iq * D + (m + 1) * 128].bitcast(FP8)
                                    rh = (fsh if pss == 0 else fsl)[:, iq * 1024 + tc * 512:iq * 1024 + tc * 512 + 512]
                                    last = (iq == 3 and pss == 1)
                                    p.em("tensor", _mm(psd[b][:], lh, rh,
                                                       iq == 0 and pss == 0, last,
                                                       s4MD if last else None))
                            n_dn += 1
                            aslc = slice(m * 1024 + tc * 512, m * 1024 + tc * 512 + 512)
                            p.wait("vector", s4MD, n_dn)
                            p.em("vector", lambda e, b=b, aslc=aslc: e.tensor_tensor(
                                acc[:, aslc], acc[:, aslc], psd[b][:], op=ALU.add).then_inc(s4DA, 1))
                            n_da += 1
                p.install_into(blk4)
            blk4.__exit__(None, None, None)
            blk4b = nc.Block()
            blk4b.__enter__()
            with ExitStack() as lnsc:
                p = _P()
                ytm = lnsc.enter_context(nc.sbuf_tensor(f"ytm{half}", [128, D], F32))
                x1ft = [lnsc.enter_context(nc.sbuf_tensor(f"x1ft{half}{i}", [128, D], F32))
                        for i in range(2)]
                yst = [lnsc.enter_context(nc.sbuf_tensor(f"yst{half}{i}", [128, D], F32))
                       for i in range(2)]
                stat4 = lnsc.enter_context(nc.sbuf_tensor(f"stat4_{half}", [128, 8], F32))
                tr4 = [lnsc.enter_context(nc.psum_tensor(f"tr4_{half}{i}", [128, 512], F32))
                       for i in range(2)]
                S1c, NMc, S2c, M2c, VRc, IVc, SDc = (stat4[:, i:i + 1] for i in range(7))
                p.wait("vector", s4DA, 32 * NG)
                for m in range(16):
                    p.em("vector", lambda e, m=m: e.tensor_scalar(
                        acc[:, m * 1024:(m + 1) * 1024], acc[:, m * 1024:(m + 1) * 1024],
                        dssb[:, m:m + 1], None, op0=ALU.mult).then_inc(s4DS, 1))
                if dbg >= 2 and half == 0:
                    p.wait("sync", s4DS, 16)
                    p.em("sync", lambda e: e.dma_start(out=accdbg[:], in_=acc[:]).then_inc(sDBG, 16))
                n_t4 = 0
                for tt in range(8):
                    p.wait("sync", s4TE, 4 * (tt - 1))
                    p.em("sync", lambda e, tt=tt: e.dma_start(
                        out=x1ft[tt % 2][:],
                        in_=_ds(x1f, half * 1024 + tt * 128, 128, D)).then_inc(s4XF, 16))
                    if tt == 0:
                        p.wait("tensor", s4DS, 16)
                    for mb in range(4):
                        ti = n_t4
                        p.wait("tensor", s4TE, ti - 1)
                        for j in range(4):
                            m = mb * 4 + j
                            p.em("tensor", lambda e, i=ti % 2, m=m, j=j, tt=tt, lastj=(j == 3): (
                                e.transpose(tr4[i][:, j * 128:(j + 1) * 128],
                                            acc[:, m * 1024 + tt * 128:m * 1024 + (tt + 1) * 128],
                                            idf[:]).then_inc(s4TT, 1) if lastj else
                                e.transpose(tr4[i][:, j * 128:(j + 1) * 128],
                                            acc[:, m * 1024 + tt * 128:m * 1024 + (tt + 1) * 128],
                                            idf[:])))
                        n_t4 += 1
                        p.wait("vector", s4TT, n_t4)
                        if mb == 0:
                            p.wait("vector", s4XF, 16 * (tt + 1))
                        p.em("vector", lambda e, i=ti % 2, mb=mb, tt=tt: e.tensor_tensor(
                            ytm[:, mb * 512:(mb + 1) * 512], tr4[i][:],
                            x1ft[tt % 2][:, mb * 512:(mb + 1) * 512], op=ALU.add).then_inc(s4TE, 1))
                    if dbg >= 2 and half == 0 and tt == 0:
                        p.wait("sync", s4TE, 16)
                        p.em("sync", lambda e: e.dma_start(out=xftdbg[:], in_=x1ft[0][:]).then_inc(sDBG, 16))
                        p.em("sync", lambda e: e.dma_start(out=ytmdbg[:], in_=ytm[:]).then_inc(sDBG, 16))
                    p.em("vector", lambda e: e.reduce_sum(S1c, ytm[:], axis=AX.X))
                    p.em("vector", lambda e: e.tensor_scalar(
                        NMc, S1c, nrd_col, None, op0=ALU.mult).then_inc(s4DS, 1))
                    p.wait("scalar", s4DS, 16 + 2 * tt + 1)
                    p.wait("scalar", s4Y, 16 * (tt - 1))
                    p.em("scalar", lambda e, b=tt % 2: e.activation(
                        out=yst[b][:], in_=ytm[:], func=AF.Square, accum_out=S2c).then_inc(s4SQ, 1))
                    p.wait("vector", s4SQ, tt + 1)
                    p.em("vector", lambda e: e.tensor_tensor(M2c, NMc, NMc, op=ALU.mult))
                    p.em("vector", lambda e: e.scalar_tensor_tensor(
                        VRc, S2c, rd_col, M2c, op0=ALU.mult, op1=ALU.subtract).then_inc(s4DS, 1))
                    p.wait("scalar", s4DS, 16 + 2 * tt + 2)
                    p.em("scalar", lambda e: e.activation(
                        out=SDc, in_=VRc, func=AF.Sqrt, bias=eps_col, scale=1.0).then_inc(s4IV, 1))
                    p.wait("vector", s4IV, tt + 1)
                    p.em("vector", lambda e: e.reciprocal(IVc, SDc).then_inc(s4RC, 1))
                    p.wait("vector", s4RC, tt + 1)
                    p.em("vector", lambda e, b=tt % 2: e.tensor_scalar(
                        yst[b][:], ytm[:], NMc, IVc, op0=ALU.add, op1=ALU.mult).then_inc(s4LN, 1))
                    p.wait("sync", s4LN, tt + 1)
                    p.em("sync", lambda e, tt=tt, b=tt % 2: e.dma_start(
                        out=_ds(y_out, half * 1024 + tt * 128, 128, D),
                        in_=yst[b][:]).then_inc(s4Y, 16))
                p.wait("sync", s4Y, 16 * 8)
                p.install_into(blk4b)
            blk4b.__exit__(None, None, None)

    return nc


def _prep_inputs(kw):
    f32 = np.float32
    q = np.ascontiguousarray(np.asarray(kw["L0_attn_query"], f32))
    com = {
        "wq": np.ascontiguousarray(np.asarray(kw["L0_attn_w_q"], f32)),
        "wk": np.ascontiguousarray(np.asarray(kw["L0_attn_w_k"], f32)),
        "wv": np.ascontiguousarray(np.asarray(kw["L0_attn_w_v"], f32)),
        "wo": np.ascontiguousarray(np.asarray(kw["L0_attn_w_o"], f32)),
        "gw": np.ascontiguousarray(np.asarray(kw["L0_ffn_gate_w"], f32)),
        "uw": np.ascontiguousarray(np.asarray(kw["L0_ffn_up_w"], f32)),
        "dw": np.ascontiguousarray(np.asarray(kw["L0_ffn_down_w"], f32)),
    }
    bq = np.asarray(kw["L0_attn_b_q"], f32).reshape(KD, 128).T
    bk = np.asarray(kw["L0_attn_b_k"], f32).reshape(KD, 128).T
    bv = np.asarray(kw["L0_attn_b_v"], f32).reshape(KD, 128).T
    com["bqkv"] = np.ascontiguousarray(np.concatenate([bq, bk, bv], axis=1))
    com["bo_row"] = np.ascontiguousarray(np.asarray(kw["L0_attn_b_o"], f32).reshape(1, D))
    com["gsc"] = np.ascontiguousarray(np.asarray(kw["L0_ffn_gate_scale"], f32).reshape(KI, 128).T)
    com["usc"] = np.ascontiguousarray(np.asarray(kw["L0_ffn_up_scale"], f32).reshape(KI, 128).T)
    com["dsc"] = np.ascontiguousarray(np.asarray(kw["L0_ffn_down_scale"], f32).reshape(KD, 128).T)
    com["idf"] = np.eye(128, dtype=f32)
    com["idb"] = np.eye(128, dtype=ml_dtypes.bfloat16)
    r = np.arange(128)
    com["maskneg"] = np.where(r[:, None] >= r[None, :], 0.0, -60000.0).astype(f32)
    com["ones_row"] = np.ones((1, 128), f32)
    cv = np.zeros((128, NCONST), f32)
    cv[:, C_S] = S_ATTN
    cv[:, C_NRD] = -1.0 / D
    cv[:, C_RD] = 1.0 / D
    cv[:, C_EPS] = 1e-5
    com["cvec"] = cv
    in_maps = []
    for c in range(8):
        m = dict(com)
        m["x"] = np.ascontiguousarray(q[c])
        in_maps.append(m)
    return in_maps


def kernel(**kw):
    nc = build_nc(upto=4, dbg=False)
    in_maps = _prep_inputs(kw)
    res = run_bass_kernel_spmd(nc, in_maps, core_ids=list(range(8)))
    out = np.stack([np.asarray(r["y"], np.float32) for r in res.results], axis=0)
    return out


# version 09
# speedup vs baseline: 1.1710x; 1.0557x over previous
import numpy as np
import ml_dtypes
from contextlib import ExitStack

import concourse.bass as bass
import concourse.mybir as mybir
from concourse.bass_utils import run_bass_kernel_spmd

F32 = mybir.dt.float32
BF16 = mybir.dt.bfloat16
FP8 = mybir.dt.float8e4
I8 = mybir.dt.int8
AF = mybir.ActivationFunctionType
ALU = mybir.AluOpType
AX = mybir.AxisListType

T = 2048
D = 2048
NH = 16
HS = 128
NI = 6144
KD = D // 128
KI = NI // 128
NG = 12
S_ATTN = float(1.0 / np.sqrt(np.float64(HS)))

C_S, C_NRD, C_RD, C_EPS, NCONST = 0, 1, 2, 3, 4


def _ds(t, r0, nr, width, c0=0, ncols=None):
    if ncols is None:
        ncols = width
    return bass.AP(t, r0 * width + c0, [[width, nr], [1, ncols]])


def _d3(t, width, r0, nk, c0, ncols, kstride=128):
    return bass.AP(t, r0 * width + c0, [[width, 128], [kstride * width, nk], [1, ncols]])


class _P:

    def __init__(self):
        self.s = {k: [] for k in ("sync", "tensor", "vector", "scalar", "gpsimd")}

    def em(self, eng, fn):
        self.s[eng].append(fn)

    def wait(self, eng, sem, val):
        if val > 0:
            self.s[eng].append(lambda e, s=sem, v=val: e.wait_ge(s, v))

    def install(self, nc):
        with nc.Block() as blk:
            self.install_into(blk)

    def install_into(self, blk):
        for name, fns in self.s.items():
            if fns:
                getattr(blk, name)(lambda e, fns=fns: [f(e) for f in fns])


def _mm(ps_ap, lh, rh, st, sp, sem=None):
    if sem is None:
        return lambda e, o=ps_ap, a=lh, b=rh, st=st, sp=sp: e.matmul(o, a, b, start=st, stop=sp)
    return lambda e, o=ps_ap, a=lh, b=rh, st=st, sp=sp, s=sem: e.matmul(
        o, a, b, start=st, stop=sp).then_inc(s, 1)


def build_nc(upto=4, dbg=False):
    nc = bass.Bass("TRN2", target_bir_lowering=False, detect_race_conditions=False)

    x_in = nc.dram_tensor("x", [T, D], F32, kind="ExternalInput")
    wq = nc.dram_tensor("wq", [D, D], F32, kind="ExternalInput")
    wk = nc.dram_tensor("wk", [D, D], F32, kind="ExternalInput")
    wv = nc.dram_tensor("wv", [D, D], F32, kind="ExternalInput")
    wo = nc.dram_tensor("wo", [D, D], F32, kind="ExternalInput")
    bqkv = nc.dram_tensor("bqkv", [128, 3 * KD], F32, kind="ExternalInput")
    bo_row = nc.dram_tensor("bo_row", [1, D], F32, kind="ExternalInput")
    gw = nc.dram_tensor("gw", [D, NI], F32, kind="ExternalInput")
    uw = nc.dram_tensor("uw", [D, NI], F32, kind="ExternalInput")
    dw = nc.dram_tensor("dw", [NI, D], F32, kind="ExternalInput")
    gs_in = nc.dram_tensor("gsc", [128, KI], F32, kind="ExternalInput")
    us_in = nc.dram_tensor("usc", [128, KI], F32, kind="ExternalInput")
    ds_in = nc.dram_tensor("dsc", [128, KD], F32, kind="ExternalInput")
    idf_in = nc.dram_tensor("idf", [128, 128], F32, kind="ExternalInput")
    idb_in = nc.dram_tensor("idb", [128, 128], BF16, kind="ExternalInput")
    mask_in = nc.dram_tensor("maskneg", [128, 128], F32, kind="ExternalInput")
    ones_row_in = nc.dram_tensor("ones_row", [1, 128], F32, kind="ExternalInput")
    cvec_in = nc.dram_tensor("cvec", [128, NCONST], F32, kind="ExternalInput")

    y_out = nc.dram_tensor("y", [T, D], F32, kind="ExternalOutput")

    sk = "ExternalOutput" if dbg else "Internal"
    qh = nc.dram_tensor("qh", [D, T], BF16, kind=sk)
    ql = nc.dram_tensor("ql", [D, T], BF16, kind=sk)
    kh = nc.dram_tensor("kh", [D, T], BF16, kind=sk)
    kl = nc.dram_tensor("kl", [D, T], BF16, kind=sk)
    vh = nc.dram_tensor("vh", [D, T], BF16, kind=sk)
    vl = nc.dram_tensor("vl", [D, T], BF16, kind=sk)
    ah = nc.dram_tensor("ah", [D, T], BF16, kind=sk)
    al = nc.dram_tensor("al", [D, T], BF16, kind=sk)
    x1f = nc.dram_tensor("x1f", [T, D], F32, kind=sk)
    pdbg = nc.dram_tensor("pdbg", [128, 8 * T], F32, kind="ExternalOutput") if dbg >= 2 else None
    vtdbg = nc.dram_tensor("vtdbg", [128, 1024], BF16, kind="ExternalOutput") if dbg >= 2 else None
    stdbg = nc.dram_tensor("stdbg", [128, 32], F32, kind="ExternalOutput") if dbg >= 2 else None
    accdbg = nc.dram_tensor("accdbg", [128, 16 * 1024], F32, kind="ExternalOutput") if dbg >= 2 else None
    xftdbg = nc.dram_tensor("xftdbg", [128, D], F32, kind="ExternalOutput") if dbg >= 2 else None
    ytmdbg = nc.dram_tensor("ytmdbg", [128, D], F32, kind="ExternalOutput") if dbg >= 2 else None
    ptdbg = nc.dram_tensor("ptdbg", [128, 1024], BF16, kind="ExternalOutput") if dbg >= 2 else None
    x1h = nc.dram_tensor("x1h", [D, T], BF16, kind=sk)
    x1l = nc.dram_tensor("x1l", [D, T], BF16, kind=sk)

    ctx = nc.ctx

    idf = ctx.enter_context(nc.sbuf_tensor("c_idf", [128, 128], F32))
    idb = ctx.enter_context(nc.sbuf_tensor("c_idb", [128, 128], BF16))
    masksb = ctx.enter_context(nc.sbuf_tensor("c_mask", [128, 128], F32))
    onesr = ctx.enter_context(nc.sbuf_tensor("c_onesr", [1, 128], F32))
    cvec = ctx.enter_context(nc.sbuf_tensor("c_cvec", [128, NCONST], F32))
    bqkv_sb = ctx.enter_context(nc.sbuf_tensor("c_bqkv", [128, 3 * KD], F32))
    bor_sb = ctx.enter_context(nc.sbuf_tensor("c_bor", [1, D], F32))
    gssb = ctx.enter_context(nc.sbuf_tensor("c_gs", [128, KI], F32))
    ussb = ctx.enter_context(nc.sbuf_tensor("c_us", [128, KI], F32))
    dssb = ctx.enter_context(nc.sbuf_tensor("c_ds", [128, KD], F32))

    s_col = cvec[:, C_S:C_S + 1]
    nrd_col = cvec[:, C_NRD:C_NRD + 1]
    rd_col = cvec[:, C_RD:C_RD + 1]
    eps_col = cvec[:, C_EPS:C_EPS + 1]

    s_c = ctx.enter_context(nc.semaphore("s_c"))
    sDBG = ctx.enter_context(nc.semaphore("sDBG")) if dbg >= 2 else None
    p0 = _P()
    consts = [(idf, idf_in), (idb, idb_in), (masksb, mask_in), (onesr, ones_row_in),
              (cvec, cvec_in), (bqkv_sb, bqkv), (bor_sb, bo_row),
              (gssb, gs_in), (ussb, us_in), (dssb, ds_in)]
    for dst, src in consts:
        p0.em("sync", lambda e, d=dst, s=src: e.dma_start(out=d[:], in_=s[:]).then_inc(s_c, 16))
    p0.wait("sync", s_c, 16 * len(consts))
    p0.install(nc)

    with ExitStack() as ph1:
        xhi = ph1.enter_context(nc.sbuf_tensor("xhi", [128, KD * T], BF16))
        xlo = ph1.enter_context(nc.sbuf_tensor("xlo", [128, KD * T], BF16))
        with ExitStack() as sc:
            xtok = [sc.enter_context(nc.sbuf_tensor(f"xtok{i}", [128, D], F32)) for i in range(2)]
            wstage = [sc.enter_context(nc.sbuf_tensor(f"wst{i}", [128, KD * 128], F32)) for i in range(2)]
            whi = [sc.enter_context(nc.sbuf_tensor(f"whi{i}", [128, KD * 128], BF16)) for i in range(2)]
            wlo = [sc.enter_context(nc.sbuf_tensor(f"wlo{i}", [128, KD * 128], BF16)) for i in range(2)]
            sthi = [sc.enter_context(nc.sbuf_tensor(f"sthi{i}", [128, T], BF16)) for i in range(2)]
            stlo = [sc.enter_context(nc.sbuf_tensor(f"stlo{i}", [128, T], BF16)) for i in range(2)]
            ptr = [sc.enter_context(nc.psum_tensor(f"p1tr{i}", [128, 512], F32)) for i in range(2)]
            psB = [sc.enter_context(nc.psum_tensor(f"p1mm{i}", [128, 512], F32)) for i in range(4)]
            sA_x = sc.enter_context(nc.semaphore("sA_x"))
            sA_tr = sc.enter_context(nc.semaphore("sA_tr"))
            sA_ev = sc.enter_context(nc.semaphore("sA_ev"))
            sB_w = sc.enter_context(nc.semaphore("sB_w"))
            sB_sp = sc.enter_context(nc.semaphore("sB_sp"))
            sB_mm = sc.enter_context(nc.semaphore("sB_mm"))
            sB_eh = sc.enter_context(nc.semaphore("sB_eh"))
            sB_el = sc.enter_context(nc.semaphore("sB_el"))
            sB_o = sc.enter_context(nc.semaphore("sB_o"))
            p = _P()
            mats = [(wq, qh, ql), (wk, kh, kl), (wv, vh, vl)]

            for t in range(16):
                p.wait("sync", sA_tr, 4 * (t - 1))
                p.em("sync", lambda e, t=t: e.dma_start(
                    out=xtok[t % 2][:], in_=_ds(x_in, t * 128, 128, D)).then_inc(sA_x, 16))
            for idx in range(48):
                mat, wc = divmod(idx, 16)
                p.wait("gpsimd", sB_sp, idx - 1)
                p.em("gpsimd", lambda e, w_t=mats[mat][0], wc=wc, idx=idx: e.dma_start(
                    out=wstage[idx % 2][:], in_=_d3(w_t, D, 0, KD, wc * 128, 128)).then_inc(sB_w, 16))

            def emit_mm_group(idx, tc):
                tile = idx * 4 + tc
                p.wait("tensor", sB_sp, idx + 1)
                p.wait("tensor", sA_ev, 16 * (tc + 1) if idx == 0 else 0)
                p.wait("tensor", sB_el, tile - 3)
                for kc in range(16):
                    for ps in range(3):
                        lh = (whi if ps < 2 else wlo)[idx % 2][:, kc * 128:(kc + 1) * 128]
                        rh = (xhi if ps != 1 else xlo)[:, kc * T + tc * 512:kc * T + tc * 512 + 512]
                        last = (kc == 15 and ps == 2)
                        p.em("tensor", _mm(psB[tile % 4][:], lh, rh,
                                           kc == 0 and ps == 0, last, sB_mm if last else None))

            def emit_split(idx):
                p.wait("vector", sB_w, 16 * (idx + 1))
                p.wait("vector", sB_mm, 4 * (idx - 1))
                p.em("vector", lambda e, i=idx: e.tensor_copy(whi[i % 2][:], wstage[i % 2][:]))
                p.em("vector", lambda e, i=idx: e.tensor_tensor(
                    wlo[i % 2][:], wstage[i % 2][:], whi[i % 2][:], op=ALU.subtract).then_inc(sB_sp, 1))

            emit_split(0)
            emit_split(1)
            n_trb = 0
            for t in range(16):
                p.wait("tensor", sA_x, 16 * (t + 1))
                for fb in range(4):
                    bi = n_trb % 2
                    p.wait("tensor", sA_ev, n_trb - 1)
                    for j in range(4):
                        f = fb * 4 + j
                        p.em("tensor", lambda e, bi=bi, t=t, f=f, j=j, lastj=(j == 3): (
                            e.transpose(ptr[bi][:, j * 128:(j + 1) * 128],
                                        xtok[t % 2][:, f * 128:(f + 1) * 128],
                                        idf[:]).then_inc(sA_tr, 1) if lastj else
                            e.transpose(ptr[bi][:, j * 128:(j + 1) * 128],
                                        xtok[t % 2][:, f * 128:(f + 1) * 128], idf[:])))
                    n_trb += 1
                    p.wait("vector", sA_tr, n_trb)
                    ob = 4 * (t % 4) // 4
                    hi_ap = bass.AP(xhi, fb * 4 * T + t * 128, [[KD * T, 128], [T, 4], [1, 128]])
                    lo_ap = bass.AP(xlo, fb * 4 * T + t * 128, [[KD * T, 128], [T, 4], [1, 128]])
                    pr_ap = bass.AP(ptr[bi], 0, [[512, 128], [128, 4], [1, 128]])
                    p.em("vector", lambda e, hi_ap=hi_ap, pr_ap=pr_ap: e.tensor_copy(hi_ap, pr_ap))
                    p.em("vector", lambda e, lo_ap=lo_ap, pr_ap=pr_ap, hi_ap=hi_ap: e.tensor_tensor(
                        lo_ap, pr_ap, hi_ap, op=ALU.subtract).then_inc(sA_ev, 1))

            def emit_evacs(idx):
                mat, wc = divmod(idx, 16)
                b_ap = bqkv_sb[:, mat * KD + wc:mat * KD + wc + 1]
                for tc in range(4):
                    tile = idx * 4 + tc
                    p.wait("vector", sB_eh, tile + 1)
                    p.em("vector", lambda e, i=idx, tc=tc, tile=tile, b=b_ap: e.scalar_tensor_tensor(
                        stlo[i % 2][:, tc * 512:(tc + 1) * 512], psB[tile % 4][:], b,
                        sthi[i % 2][:, tc * 512:(tc + 1) * 512],
                        op0=ALU.add, op1=ALU.subtract).then_inc(sB_el, 1))

            for idx in range(2, 48):
                emit_split(idx)
                emit_evacs(idx - 2)
            emit_evacs(46)
            emit_evacs(47)

            for idx in range(48):
                for tc in range(4):
                    emit_mm_group(idx, tc)

            for tile in range(192):
                idx, tc = divmod(tile, 4)
                mat, wc = divmod(idx, 16)
                b_ap = bqkv_sb[:, mat * KD + wc:mat * KD + wc + 1]
                p.wait("scalar", sB_mm, tile + 1)
                if tc == 0:
                    p.wait("scalar", sB_o, 32 * (idx - 1))
                p.em("scalar", lambda e, i=idx, tc=tc, tile=tile, b=b_ap: e.activation(
                    out=sthi[i % 2][:, tc * 512:(tc + 1) * 512], in_=psB[tile % 4][:],
                    func=AF.Identity, bias=b, scale=1.0).then_inc(sB_eh, 1))

            for idx in range(48):
                mat, wc = divmod(idx, 16)
                p.wait("sync", sB_el, 4 * (idx + 1))
                p.em("sync", lambda e, i=idx, t=mats[mat][1], wc=wc: e.dma_start(
                    out=_ds(t, wc * 128, 128, T), in_=sthi[i % 2][:]).then_inc(sB_o, 16))
                p.em("sync", lambda e, i=idx, t=mats[mat][2], wc=wc: e.dma_start(
                    out=_ds(t, wc * 128, 128, T), in_=stlo[i % 2][:]).then_inc(sB_o, 16))
            p.wait("sync", sB_o, 16 * 96)
            p.install(nc)

    if upto < 2:
        return nc

    with ExitStack() as sc:
        qhb = [sc.enter_context(nc.sbuf_tensor(f"qhb{i}", [128, T], BF16)) for i in range(2)]
        qlb = [sc.enter_context(nc.sbuf_tensor(f"qlb{i}", [128, T], BF16)) for i in range(2)]
        khb = [sc.enter_context(nc.sbuf_tensor(f"khb{i}", [128, T], BF16)) for i in range(2)]
        klb = [sc.enter_context(nc.sbuf_tensor(f"klb{i}", [128, T], BF16)) for i in range(2)]
        vhb = [sc.enter_context(nc.sbuf_tensor(f"vhb{i}", [128, T], BF16)) for i in range(2)]
        vlb = [sc.enter_context(nc.sbuf_tensor(f"vlb{i}", [128, T], BF16)) for i in range(2)]
        P_sb = [sc.enter_context(nc.sbuf_tensor(f"Psb{i}", [128, T], F32)) for i in range(8)]
        vth = sc.enter_context(nc.sbuf_tensor("vth", [128, T], BF16))
        vtl = sc.enter_context(nc.sbuf_tensor("vtl", [128, T], BF16))
        vf32 = sc.enter_context(nc.sbuf_tensor("vf32", [128, 512], F32))
        pth = sc.enter_context(nc.sbuf_tensor("pth", [128, 2048], BF16))
        ptl = sc.enter_context(nc.sbuf_tensor("ptl", [128, 2048], BF16))
        asth = [sc.enter_context(nc.sbuf_tensor(f"asth{i}", [128, 1024], BF16)) for i in range(2)]
        astl = [sc.enter_context(nc.sbuf_tensor(f"astl{i}", [128, 1024], BF16)) for i in range(2)]
        nmax = sc.enter_context(nc.sbuf_tensor("nmax", [128, 8], F32))
        nmaxs = sc.enter_context(nc.sbuf_tensor("nmaxs", [128, 8], F32))
        srow = sc.enter_context(nc.sbuf_tensor("srow", [128, 8], F32))
        rrow = sc.enter_context(nc.sbuf_tensor("rrow", [128, 8], F32))
        mxp = sc.enter_context(nc.sbuf_tensor("mxp", [128, 4], F32))
        sH = sc.enter_context(nc.semaphore("sH"))
        sLG = sc.enter_context(nc.semaphore("sLG"))
        sMX = sc.enter_context(nc.semaphore("sMX"))
        sEX = sc.enter_context(nc.semaphore("sEX"))
        sPN = sc.enter_context(nc.semaphore("sPN"))
        sVF = sc.enter_context(nc.semaphore("sVF"))
        sTXt = sc.enter_context(nc.semaphore("sTXt"))
        sTH = sc.enter_context(nc.semaphore("sTH"))
        sTXe = sc.enter_context(nc.semaphore("sTXe"))
        sPV = sc.enter_context(nc.semaphore("sPV"))
        sAH = sc.enter_context(nc.semaphore("sAH"))
        sAE = sc.enter_context(nc.semaphore("sAE"))
        sAO = sc.enter_context(nc.semaphore("sAO"))
        sRC = sc.enter_context(nc.semaphore("sRC"))

        n_h = 0
        n_lgc = n_mx = n_ex = n_pn = n_rc = 0
        n_vf = 0
        n_tb = 0
        n_th = 0
        n_txe = 0
        n_pv = 0
        tb_after = {}

        blk2 = nc.Block()
        blk2.__enter__()
        for h in range(16):
            p = _P()
            p.wait("sync", sAE, 2 * (h - 1))
            for dst, src2 in [(qhb, qh), (qlb, ql), (khb, kh), (klb, kl), (vhb, vh), (vlb, vl)]:
                p.em("sync", lambda e, d=dst[h % 2], s=src2, h=h: e.dma_start(
                    out=d[:], in_=_ds(s, h * 128, 128, T)).then_inc(sH, 16))
            n_h += 6
            p.install_into(blk2)

            for g in range(2):
                gidx = 2 * h + g
                kbmax = g * 8 + 8
                with ExitStack() as hs:
                    p = _P()
                    lgA = hs.enter_context(nc.psum_tensor(f"lgA{gidx}", [128, T], F32))
                    lgB = hs.enter_context(nc.psum_tensor(f"lgB{gidx}", [128, T], F32))
                    exbase = n_ex
                    p.wait("tensor", sH, 16 * n_h)
                    p.wait("tensor", sAE, gidx)
                    if gidx >= 1:
                        p.wait("scalar", sTXt, tb_after[gidx - 1])
                    pend_norm = None
                    for qi in range(8):
                        qg = g * 8 + qi
                        ncols = (qg + 1) * 128
                        lg = lgA if qi % 2 == 0 else lgB
                        nchunks = (ncols + 511) // 512
                        p.wait("tensor", sEX, exbase + qi - 1)
                        ch_base = n_lgc
                        for cb in range(nchunks):
                            cw = min(512, ncols - cb * 512)
                            for ps in range(3):
                                lh = (qhb if ps != 1 else qlb)[h % 2][:, qg * 128:(qg + 1) * 128]
                                rh = (khb if ps != 2 else klb)[h % 2][:, cb * 512:cb * 512 + cw]
                                p.em("tensor", _mm(lg[:, cb * 512:cb * 512 + cw], lh, rh,
                                                   ps == 0, ps == 2, sLG if ps == 2 else None))
                            n_lgc += 1
                        for cb in range(nchunks):
                            cw = min(512, ncols - cb * 512)
                            p.wait("vector", sLG, ch_base + cb + 1)
                            if cb == nchunks - 1:
                                p.em("vector", lambda e, lg=lg, nc_=ncols: e.tensor_tensor(
                                    lg[:, nc_ - 128:nc_], lg[:, nc_ - 128:nc_], masksb[:], op=ALU.add))
                            p.em("vector", lambda e, lg=lg, cb=cb, cw=cw: e.reduce_max(
                                mxp[:, cb:cb + 1], lg[:, cb * 512:cb * 512 + cw], axis=AX.X))
                        p.em("vector", lambda e, qi=qi, nch=nchunks: e.reduce_max(
                            nmax[:, qi:qi + 1], mxp[:, 0:nch], axis=AX.X, negate=True))
                        p.em("vector", lambda e, qi=qi: e.tensor_scalar(
                            nmaxs[:, qi:qi + 1], nmax[:, qi:qi + 1], s_col, None,
                            op0=ALU.mult).then_inc(sMX, 1))
                        n_mx += 1
                        p.wait("scalar", sMX, n_mx)
                        p.em("scalar", lambda e, lg=lg, nc_=ncols, qi=qi: e.activation(
                            out=P_sb[qi][:, 0:nc_], in_=lg[:, 0:nc_], func=AF.Exp,
                            bias=nmaxs[:, qi:qi + 1], scale=s_col,
                            accum_out=srow[:, qi:qi + 1]).then_inc(sEX, 1))
                        n_ex += 1
                        if pend_norm is not None:
                            pqi, pcols, pex = pend_norm
                            p.wait("vector", sEX, pex)
                            n_rc += 1
                            p.em("vector", lambda e, qi=pqi: e.reciprocal(
                                rrow[:, qi:qi + 1], srow[:, qi:qi + 1]).then_inc(sRC, 1))
                            p.wait("vector", sRC, n_rc)
                            p.em("vector", lambda e, qi=pqi, nc_=pcols: e.tensor_scalar(
                                P_sb[qi][:, 0:nc_], P_sb[qi][:, 0:nc_], rrow[:, qi:qi + 1],
                                None, op0=ALU.mult).then_inc(sPN, 1))
                            n_pn += 1
                        pend_norm = (qi, ncols, n_ex)
                    pqi, pcols, pex = pend_norm
                    p.wait("vector", sEX, pex)
                    n_rc += 1
                    p.em("vector", lambda e, qi=pqi: e.reciprocal(
                        rrow[:, qi:qi + 1], srow[:, qi:qi + 1]).then_inc(sRC, 1))
                    p.wait("vector", sRC, n_rc)
                    p.em("vector", lambda e, qi=pqi, nc_=pcols: e.tensor_scalar(
                        P_sb[pqi][:, 0:nc_], P_sb[pqi][:, 0:nc_], rrow[:, pqi:pqi + 1],
                        None, op0=ALU.mult).then_inc(sPN, 1))
                    n_pn += 1
                    p.install_into(blk2)
                with ExitStack() as hs:
                    p = _P()
                    pv = hs.enter_context(nc.psum_tensor(f"pv{gidx}", [128, 1024], F32))
                    trB = [hs.enter_context(nc.psum_tensor(f"trB{i}_{gidx}", [128, 512], F32))
                           for i in range(2)]
                    p.wait("tensor", sEX, exbase + 7)
                    p.wait("tensor", sAE, gidx)
                    last_half_kb = {0: g * 8 + 3, 1: kbmax - 1}
                    pn_base = n_pn - 8

                    def tr_batch(srcs, dsth, dstl, dc0, width, pn_need=0):
                        nonlocal n_tb, n_txe
                        bi = n_tb % 2
                        p.wait("tensor", sTXe, n_tb - 1)
                        p.wait("tensor", sPN, pn_need)
                        for j, s_ap in enumerate(srcs):
                            p.em("tensor", lambda e, bi=bi, j=j, s_ap=s_ap: e.transpose(
                                trB[bi][:, j * 128:(j + 1) * 128], s_ap,
                                idf[:]).then_inc(sTXt, 1) if j == len(srcs) - 1 else
                                e.transpose(trB[bi][:, j * 128:(j + 1) * 128], s_ap, idf[:]))
                        n_tb += 1
                        p.wait("vector", sTXt, n_tb)
                        p.em("vector", lambda e, bi=bi, w=width, dc0=dc0, dsth=dsth: e.tensor_copy(
                            dsth[:, dc0:dc0 + w], trB[bi][:, 0:w]))
                        p.em("vector", lambda e, bi=bi, w=width, dc0=dc0, dsth=dsth, dstl=dstl: e.tensor_tensor(
                            dstl[:, dc0:dc0 + w], trB[bi][:, 0:w], dsth[:, dc0:dc0 + w],
                            op=ALU.subtract).then_inc(sTXe, 1))
                        n_txe += 1

                    nb0 = g * 8
                    for vb in range(2):
                        kb0 = nb0 + vb * 4
                        p.wait("vector", sTXt, n_tb)
                        p.em("vector", lambda e, kb0=kb0, hh=h % 2: e.tensor_tensor(
                            vf32[:], vhb[hh][:, kb0 * 128:kb0 * 128 + 512],
                            vlb[hh][:, kb0 * 128:kb0 * 128 + 512], op=ALU.add).then_inc(sVF, 1))
                        n_vf += 1
                        p.wait("tensor", sVF, n_vf)
                        tr_batch([vf32[:, j * 128:(j + 1) * 128] for j in range(4)],
                                 vth, vtl, kb0 * 128, 512)
                    def emit_pv(kb, txe_need):
                        qi_min = max(0, kb - g * 8)
                        po = (kb % 2) * 1024
                        p.wait("tensor", sTXe, txe_need)
                        for half in range(2):
                            c0 = max(qi_min * 128, half * 512)
                            c1 = (half + 1) * 512
                            if c0 >= c1:
                                continue
                            for ps in range(3):
                                lh = (vth if ps != 1 else vtl)[:, kb * 128:(kb + 1) * 128]
                                rh = (pth if ps != 2 else ptl)[:, po + c0:po + c1]
                                last = (kb == kbmax - 1 and half == 1 and ps == 2)
                                p.em("tensor", _mm(pv[:, c0:c1], lh, rh,
                                                   kb == 0 and ps == 0,
                                                   kb == last_half_kb[half] and ps == 2,
                                                   sPV if last else None))

                    prev_pv = None
                    for kb in range(kbmax):
                        qi_min = max(0, kb - g * 8)
                        po = (kb % 2) * 1024
                        qq = qi_min
                        while qq < 8:
                            nb = min(4, 8 - qq)
                            tr_batch([P_sb[qi][:, kb * 128:(kb + 1) * 128]
                                      for qi in range(qq, qq + nb)],
                                     pth, ptl, po + qq * 128, nb * 128,
                                     pn_need=pn_base + qq + nb)
                            qq += nb
                        if prev_pv is not None:
                            emit_pv(*prev_pv)
                        prev_pv = (kb, n_txe)
                    emit_pv(*prev_pv)
                    n_pv += 1
                    tb_after[gidx] = n_tb
                    p.wait("scalar", sPV, n_pv)
                    p.wait("scalar", sAO, 32 * (gidx - 1))
                    p.em("scalar", lambda e, b=gidx % 2: e.activation(
                        out=asth[b][:], in_=pv[:], func=AF.Identity, bias=0.0,
                        scale=1.0).then_inc(sAH, 1))
                    p.wait("vector", sAH, gidx + 1)
                    p.em("vector", lambda e, b=gidx % 2: e.tensor_tensor(
                        astl[b][:], pv[:], asth[b][:], op=ALU.subtract).then_inc(sAE, 1))
                    p.wait("sync", sAE, gidx + 1)
                    p.em("sync", lambda e, b=gidx % 2, h=h, g=g: e.dma_start(
                        out=bass.AP(ah, h * 128 * T + g * 1024, [[T, 128], [1, 1024]]),
                        in_=asth[b][:]).then_inc(sAO, 16))
                    p.em("sync", lambda e, b=gidx % 2, h=h, g=g: e.dma_start(
                        out=bass.AP(al, h * 128 * T + g * 1024, [[T, 128], [1, 1024]]),
                        in_=astl[b][:]).then_inc(sAO, 16))
                    p.install_into(blk2)
        p = _P()
        p.wait("sync", sAO, 32 * 32)
        p.install_into(blk2)
        blk2.__exit__(None, None, None)

    if upto < 3:
        return nc

    with ExitStack() as sc:
        athi = sc.enter_context(nc.sbuf_tensor("athi", [128, KD * 1024], BF16))
        atlo = sc.enter_context(nc.sbuf_tensor("atlo", [128, KD * 1024], BF16))
        x1pre = sc.enter_context(nc.sbuf_tensor("x1pre", [128, 8 * D], F32))
        wst3 = sc.enter_context(nc.sbuf_tensor("wst3", [128, KD * 256], F32))
        w3h = sc.enter_context(nc.sbuf_tensor("w3h", [128, KD * 256], BF16))
        w3l = sc.enter_context(nc.sbuf_tensor("w3l", [128, KD * 256], BF16))
        xres = sc.enter_context(nc.sbuf_tensor("xres", [128, 8 * 256], F32))
        bobc = sc.enter_context(nc.sbuf_tensor("bobc", [128, D], F32))
        x1st = [sc.enter_context(nc.sbuf_tensor("x1st0", [128, D], F32))]
        x1hst = [sc.enter_context(nc.sbuf_tensor("x1hst0", [128, D], BF16))]
        x1lst = [sc.enter_context(nc.sbuf_tensor("x1lst0", [128, D], BF16))]
        stat = sc.enter_context(nc.sbuf_tensor("stat3", [128, 8], F32))
        ps3 = [sc.enter_context(nc.psum_tensor(f"ps3_{i}", [128, 256], F32)) for i in range(4)]
        bops = sc.enter_context(nc.psum_tensor("bops", [128, 512], F32))
        tr3 = [sc.enter_context(nc.psum_tensor(f"tr3_{i}", [128, 512], F32)) for i in range(2)]
        s3A = sc.enter_context(nc.semaphore("s3A"))
        s3W = sc.enter_context(nc.semaphore("s3W"))
        s3X = sc.enter_context(nc.semaphore("s3X"))
        s3SP = sc.enter_context(nc.semaphore("s3SP"))
        s3BC = sc.enter_context(nc.semaphore("s3BC"))
        s3BCE = sc.enter_context(nc.semaphore("s3BCE"))
        s3MM = sc.enter_context(nc.semaphore("s3MM"))
        s3EV = sc.enter_context(nc.semaphore("s3EV"))
        s3SQ = sc.enter_context(nc.semaphore("s3SQ"))
        s3VR = sc.enter_context(nc.semaphore("s3VR"))
        s3IV = sc.enter_context(nc.semaphore("s3IV"))
        s3LNa = sc.enter_context(nc.semaphore("s3LNa"))
        s3TRt = sc.enter_context(nc.semaphore("s3TRt"))
        s3TRe = sc.enter_context(nc.semaphore("s3TRe"))
        s3XO = sc.enter_context(nc.semaphore("s3XO"))
        s3HC = sc.enter_context(nc.semaphore("s3HC"))

        S1c, NMc, S2c, M2c, VRc, IVc, SDc = (stat[:, i:i + 1] for i in range(7))
        p = _P()
        n_tr3 = 0
        n_hc = [0]
        for bcw in range(4):
            p.wait("tensor", s3BC, 2 * bcw)
            p.em("tensor", lambda e, bcw=bcw: e.matmul(
                bops[:, 0:512], onesr[:], bor_sb[0:1, bcw * 512:(bcw + 1) * 512],
                start=True, stop=True).then_inc(s3BC, 1))
            p.wait("vector", s3BC, 2 * bcw + 1)
            p.em("vector", lambda e, bcw=bcw: e.tensor_copy(
                bobc[:, bcw * 512:(bcw + 1) * 512], bops[:, 0:512]).then_inc(s3BC, 1))
        for grp in range(2):
            p.wait("sync", s3MM, 64 * grp)
            p.em("sync", lambda e, grp=grp: e.dma_start(
                out=athi[:], in_=_d3(ah, T, 0, KD, grp * 1024, 1024)).then_inc(s3A, 16))
            p.em("sync", lambda e, grp=grp: e.dma_start(
                out=atlo[:], in_=_d3(al, T, 0, KD, grp * 1024, 1024)).then_inc(s3A, 16))
            for wc in range(8):
                wi = grp * 8 + wc
                p.wait("sync", s3SP, wi)
                p.em("sync", lambda e, wc=wc: e.dma_start(
                    out=wst3[:], in_=_d3(wo, D, 0, KD, wc * 256, 256)).then_inc(s3W, 16))
                p.wait("sync", s3EV, 8 * wi)
                p.em("sync", lambda e, grp=grp, wc=wc: e.dma_start(
                    out=xres[:], in_=_d3(x_in, D, grp * 1024, 8, wc * 256, 256)).then_inc(s3X, 16))
                p.wait("vector", s3W, 16 * (wi + 1))
                p.wait("vector", s3MM, 8 * wi)
                p.em("vector", lambda e: e.tensor_copy(w3h[:], wst3[:]))
                p.em("vector", lambda e: e.tensor_tensor(
                    w3l[:], wst3[:], w3h[:], op=ALU.subtract).then_inc(s3SP, 1))
                p.wait("tensor", s3SP, wi + 1)
                p.wait("tensor", s3A, 32 * (grp + 1))
                for tt in range(8):
                    til = wi * 8 + tt
                    p.wait("tensor", s3EV, til - 3)
                    for kc in range(16):
                        for ps in range(3):
                            lh = (athi if ps != 1 else atlo)[:, kc * 1024 + tt * 128:kc * 1024 + (tt + 1) * 128]
                            rh = (w3h if ps != 2 else w3l)[:, kc * 256:(kc + 1) * 256]
                            last = (kc == 15 and ps == 2)
                            p.em("tensor", _mm(ps3[til % 4][:], lh, rh,
                                               kc == 0 and ps == 0, last, s3MM if last else None))
                    dstc = slice(tt * D + wc * 256, tt * D + (wc + 1) * 256)
                    p.wait("vector", s3MM, til + 1)
                    p.wait("vector", s3X, 16 * (wi + 1))
                    p.em("vector", lambda e, til=til, dstc=dstc, wc=wc: e.tensor_tensor(
                        x1pre[:, dstc], ps3[til % 4][:],
                        bobc[:, wc * 256:(wc + 1) * 256], op=ALU.add))
                    p.em("vector", lambda e, til=til, dstc=dstc, tt=tt: e.tensor_tensor(
                        x1pre[:, dstc], x1pre[:, dstc],
                        xres[:, tt * 256:(tt + 1) * 256], op=ALU.add).then_inc(s3EV, 1))
            for tt in range(8):
                ttg = grp * 8 + tt
                row = slice(tt * D, (tt + 1) * D)
                p.wait("vector", s3EV, 64 * (grp + 1))
                p.em("vector", lambda e, row=row: e.reduce_sum(S1c, x1pre[:, row], axis=AX.X))
                p.em("vector", lambda e: e.tensor_scalar(
                    NMc, S1c, nrd_col, None, op0=ALU.mult).then_inc(s3LNa, 1))
                p.wait("scalar", s3LNa, 3 * ttg + 1)
                p.wait("scalar", s3XO, 48 * ttg)
                p.em("scalar", lambda e, row=row: e.activation(
                    out=x1st[0][:], in_=x1pre[:, row], func=AF.Square,
                    accum_out=S2c).then_inc(s3SQ, 1))
                p.wait("vector", s3SQ, ttg + 1)
                p.em("vector", lambda e: e.tensor_tensor(M2c, NMc, NMc, op=ALU.mult))
                p.em("vector", lambda e: e.scalar_tensor_tensor(
                    VRc, S2c, rd_col, M2c, op0=ALU.mult, op1=ALU.subtract).then_inc(s3LNa, 1))
                p.wait("scalar", s3LNa, 3 * ttg + 2)
                p.em("scalar", lambda e: e.activation(
                    out=SDc, in_=VRc, func=AF.Sqrt, bias=eps_col, scale=1.0).then_inc(s3IV, 1))
                p.wait("vector", s3IV, ttg + 1)
                p.em("vector", lambda e: e.reciprocal(IVc, SDc).then_inc(s3VR, 1))
                p.wait("vector", s3VR, ttg + 1)
                p.em("vector", lambda e, row=row: e.tensor_scalar(
                    x1st[0][:], x1pre[:, row], NMc, IVc,
                    op0=ALU.add, op1=ALU.mult).then_inc(s3LNa, 1))
                p.wait("sync", s3LNa, 3 * ttg + 3)
                p.em("sync", lambda e, ttg=ttg: e.dma_start(
                    out=_ds(x1f, ttg * 128, 128, D), in_=x1st[0][:]).then_inc(s3XO, 16))
                p.wait("tensor", s3LNa, 3 * ttg + 3)
                for fb in range(4):
                    ti = n_tr3
                    p.wait("tensor", s3TRe, ti - 1)
                    for j in range(4):
                        f = fb * 4 + j
                        p.em("tensor", lambda e, i=ti % 2, f=f, j=j, lastj=(j == 3): (
                            e.transpose(tr3[i][:, j * 128:(j + 1) * 128],
                                        x1st[0][:, f * 128:(f + 1) * 128],
                                        idf[:]).then_inc(s3TRt, 1) if lastj else
                            e.transpose(tr3[i][:, j * 128:(j + 1) * 128],
                                        x1st[0][:, f * 128:(f + 1) * 128], idf[:])))
                    n_tr3 += 1
                    p.wait("scalar", s3TRt, n_tr3)
                    if fb == 0:
                        p.wait("scalar", s3XO, 48 * ttg)
                    p.em("scalar", lambda e, i=ti % 2, fb=fb: e.copy(
                        x1hst[0][:, fb * 512:(fb + 1) * 512], tr3[i][:]).then_inc(s3HC, 1))
                    n_hc[0] += 1
                    p.wait("vector", s3HC, n_hc[0])
                    p.em("vector", lambda e, i=ti % 2, fb=fb: e.tensor_tensor(
                        x1lst[0][:, fb * 512:(fb + 1) * 512], tr3[i][:],
                        x1hst[0][:, fb * 512:(fb + 1) * 512], op=ALU.subtract).then_inc(s3TRe, 1))
                p.wait("sync", s3TRe, n_tr3)
                p.em("sync", lambda e, ttg=ttg: e.dma_start(
                    out=_d3(x1h, T, 0, KD, ttg * 128, 128), in_=x1hst[0][:]).then_inc(s3XO, 16))
                p.em("sync", lambda e, ttg=ttg: e.dma_start(
                    out=_d3(x1l, T, 0, KD, ttg * 128, 128), in_=x1lst[0][:]).then_inc(s3XO, 16))
        p.wait("sync", s3XO, 48 * 16)
        p.install(nc)

    if upto < 4:
        return nc

    for half in range(2):
        with ExitStack() as sc:
            x1hb = sc.enter_context(nc.sbuf_tensor(f"x1hb{half}", [128, KD * 1024], BF16))
            x1lb = sc.enter_context(nc.sbuf_tensor(f"x1lb{half}", [128, KD * 1024], BF16))
            acc = sc.enter_context(nc.sbuf_tensor(f"acc{half}", [128, KD * 1024], F32))
            s4X = sc.enter_context(nc.semaphore(f"s4X{half}"))
            s4W = sc.enter_context(nc.semaphore(f"s4W{half}"))
            s4WD = sc.enter_context(nc.semaphore(f"s4WD{half}"))
            s4MG = sc.enter_context(nc.semaphore(f"s4MG{half}"))
            s4SI = sc.enter_context(nc.semaphore(f"s4SI{half}"))
            s4FU = sc.enter_context(nc.semaphore(f"s4FU{half}"))
            s4MD = sc.enter_context(nc.semaphore(f"s4MD{half}"))
            s4DA = sc.enter_context(nc.semaphore(f"s4DA{half}"))
            s4DS = sc.enter_context(nc.semaphore(f"s4DS{half}"))
            s4XF = sc.enter_context(nc.semaphore(f"s4XF{half}"))
            s4TT = sc.enter_context(nc.semaphore(f"s4TT{half}"))
            s4TE = sc.enter_context(nc.semaphore(f"s4TE{half}"))
            s4LN = sc.enter_context(nc.semaphore(f"s4LN{half}"))
            s4SQ = sc.enter_context(nc.semaphore(f"s4SQ{half}"))
            s4IV = sc.enter_context(nc.semaphore(f"s4IV{half}"))
            s4RC = sc.enter_context(nc.semaphore(f"s4RC{half}"))
            s4Y = sc.enter_context(nc.semaphore(f"s4Y{half}"))
            blk4 = nc.Block()
            blk4.__enter__()
            p = _P()
            p.em("sync", lambda e: e.dma_start(
                out=x1hb[:], in_=_d3(x1h, T, 0, KD, half * 1024, 1024)).then_inc(s4X, 16))
            p.em("sync", lambda e: e.dma_start(
                out=x1lb[:], in_=_d3(x1l, T, 0, KD, half * 1024, 1024)).then_inc(s4X, 16))
            p.em("vector", lambda e: e.memset(acc[:], 0.0))
            p.install_into(blk4)
            with ExitStack() as mmsc:
                p = _P()
                g8 = [mmsc.enter_context(nc.sbuf_tensor(f"g8_{half}{i}", [128, KD * 512], I8))
                      for i in range(2)]
                u8 = [mmsc.enter_context(nc.sbuf_tensor(f"u8_{half}{i}", [128, KD * 512], I8))
                      for i in range(2)]
                d8 = [mmsc.enter_context(nc.sbuf_tensor(f"d8_{half}0", [128, 4 * D], I8))]
                fsh = mmsc.enter_context(nc.sbuf_tensor(f"fsh{half}", [128, 4 * 1024], BF16))
                fsl = mmsc.enter_context(nc.sbuf_tensor(f"fsl{half}", [128, 4 * 1024], BF16))
                silu = [mmsc.enter_context(nc.sbuf_tensor(f"silu{half}0", [128, 512], F32))]
                fu32 = [mmsc.enter_context(nc.sbuf_tensor(f"fu32{half}0", [128, 512], F32))]
                psg = [mmsc.enter_context(nc.psum_tensor(f"psg{half}{i}", [128, 512], F32))
                       for i in range(2)]
                psu = [mmsc.enter_context(nc.psum_tensor(f"psu{half}{i}", [128, 512], F32))
                       for i in range(2)]
                psd = [mmsc.enter_context(nc.psum_tensor(f"psd{half}{i}", [128, 512], F32))
                       for i in range(2)]
                n_gu = 0
                n_dn = 0
                n_fu = 0
                n_da = 0
                for gi in range(NG):
                    p.wait("gpsimd", s4MD, 32 * (gi - 1))
                    p.em("gpsimd", lambda e, gi=gi: e.dma_start(
                        out=g8[gi % 2][:].bitcast(FP8),
                        in_=_d3(gw, NI, 0, KD, gi * 512, 512)).then_inc(s4W, 16))
                    p.em("gpsimd", lambda e, gi=gi: e.dma_start(
                        out=u8[gi % 2][:].bitcast(FP8),
                        in_=_d3(uw, NI, 0, KD, gi * 512, 512)).then_inc(s4W, 16))
                    p.wait("gpsimd", s4MD, 32 * gi)
                    p.em("gpsimd", lambda e, gi=gi: e.dma_start(
                        out=d8[0][:].bitcast(FP8),
                        in_=_d3(dw, D, gi * 512, 4, 0, D)).then_inc(s4WD, 16))
                    p.wait("tensor", s4W, 32 * (gi + 1))
                    if gi == 0:
                        p.wait("tensor", s4X, 32)
                    for iq in range(4):
                        for tc in range(2):
                            b = (iq * 2 + tc) % 2
                            step = gi * 8 + iq * 2 + tc
                            p.wait("tensor", s4FU, step - 1)
                            for mat, ps_t, sb8 in ((0, psg, g8), (1, psu, u8)):
                                for kc in range(16):
                                    for pss in range(2):
                                        lh = sb8[gi % 2][:, kc * 512 + iq * 128:kc * 512 + (iq + 1) * 128].bitcast(FP8)
                                        rh = (x1hb if pss == 0 else x1lb)[:, kc * 1024 + tc * 512:kc * 1024 + tc * 512 + 512]
                                        last = (kc == 15 and pss == 1)
                                        p.em("tensor", _mm(ps_t[b][:], lh, rh,
                                                           kc == 0 and pss == 0, last,
                                                           s4MG if last else None))
                                n_gu += 1
                            p.wait("scalar", s4MG, n_gu - 1)
                            p.wait("scalar", s4FU, n_fu)
                            gcol = gssb[:, gi * 4 + iq:gi * 4 + iq + 1]
                            p.em("scalar", lambda e, b=b, g=gcol: e.activation(
                                out=silu[0][:], in_=psg[b][:], func=AF.Silu, bias=0.0,
                                scale=g).then_inc(s4SI, 1))
                            ucol = ussb[:, gi * 4 + iq:gi * 4 + iq + 1]
                            fslc = slice(iq * 1024 + tc * 512, iq * 1024 + tc * 512 + 512)
                            p.wait("vector", s4MG, n_gu)
                            p.wait("vector", s4SI, step + 1)
                            if iq == 0 and tc == 0:
                                p.wait("vector", s4MD, 32 * gi)
                            p.em("vector", lambda e, b=b, u=ucol: e.scalar_tensor_tensor(
                                fu32[0][:], psu[b][:], u, silu[0][:], op0=ALU.mult, op1=ALU.mult))
                            p.em("vector", lambda e, fslc=fslc: e.tensor_copy(
                                fsh[:, fslc], fu32[0][:]))
                            p.em("vector", lambda e, fslc=fslc: e.tensor_tensor(
                                fsl[:, fslc], fu32[0][:], fsh[:, fslc],
                                op=ALU.subtract).then_inc(s4FU, 1))
                            n_fu += 1
                    p.wait("tensor", s4FU, n_fu)
                    p.wait("tensor", s4WD, 16 * (gi + 1))
                    for m in range(16):
                        for tc in range(2):
                            b = (m * 2 + tc) % 2
                            p.wait("tensor", s4DA, n_dn - 1)
                            for iq in range(4):
                                for pss in range(2):
                                    lh = d8[0][:, iq * D + m * 128:iq * D + (m + 1) * 128].bitcast(FP8)
                                    rh = (fsh if pss == 0 else fsl)[:, iq * 1024 + tc * 512:iq * 1024 + tc * 512 + 512]
                                    last = (iq == 3 and pss == 1)
                                    p.em("tensor", _mm(psd[b][:], lh, rh,
                                                       iq == 0 and pss == 0, last,
                                                       s4MD if last else None))
                            n_dn += 1
                            aslc = slice(m * 1024 + tc * 512, m * 1024 + tc * 512 + 512)
                            p.wait("vector", s4MD, n_dn)
                            p.em("vector", lambda e, b=b, aslc=aslc: e.tensor_tensor(
                                acc[:, aslc], acc[:, aslc], psd[b][:], op=ALU.add).then_inc(s4DA, 1))
                            n_da += 1
                p.install_into(blk4)
            blk4.__exit__(None, None, None)
            blk4b = nc.Block()
            blk4b.__enter__()
            with ExitStack() as lnsc:
                p = _P()
                ytm = lnsc.enter_context(nc.sbuf_tensor(f"ytm{half}", [128, D], F32))
                x1ft = [lnsc.enter_context(nc.sbuf_tensor(f"x1ft{half}{i}", [128, D], F32))
                        for i in range(2)]
                yst = [lnsc.enter_context(nc.sbuf_tensor(f"yst{half}{i}", [128, D], F32))
                       for i in range(2)]
                stat4 = lnsc.enter_context(nc.sbuf_tensor(f"stat4_{half}", [128, 8], F32))
                tr4 = [lnsc.enter_context(nc.psum_tensor(f"tr4_{half}{i}", [128, 512], F32))
                       for i in range(2)]
                S1c, NMc, S2c, M2c, VRc, IVc, SDc = (stat4[:, i:i + 1] for i in range(7))
                p.wait("vector", s4DA, 32 * NG)
                for m in range(16):
                    p.em("vector", lambda e, m=m: e.tensor_scalar(
                        acc[:, m * 1024:(m + 1) * 1024], acc[:, m * 1024:(m + 1) * 1024],
                        dssb[:, m:m + 1], None, op0=ALU.mult).then_inc(s4DS, 1))
                if dbg >= 2 and half == 0:
                    p.wait("sync", s4DS, 16)
                    p.em("sync", lambda e: e.dma_start(out=accdbg[:], in_=acc[:]).then_inc(sDBG, 16))
                n_t4 = 0
                for tt in range(8):
                    p.wait("sync", s4TE, 4 * (tt - 1))
                    p.em("sync", lambda e, tt=tt: e.dma_start(
                        out=x1ft[tt % 2][:],
                        in_=_ds(x1f, half * 1024 + tt * 128, 128, D)).then_inc(s4XF, 16))
                    if tt == 0:
                        p.wait("tensor", s4DS, 16)
                    for mb in range(4):
                        ti = n_t4
                        p.wait("tensor", s4TE, ti - 1)
                        for j in range(4):
                            m = mb * 4 + j
                            p.em("tensor", lambda e, i=ti % 2, m=m, j=j, tt=tt, lastj=(j == 3): (
                                e.transpose(tr4[i][:, j * 128:(j + 1) * 128],
                                            acc[:, m * 1024 + tt * 128:m * 1024 + (tt + 1) * 128],
                                            idf[:]).then_inc(s4TT, 1) if lastj else
                                e.transpose(tr4[i][:, j * 128:(j + 1) * 128],
                                            acc[:, m * 1024 + tt * 128:m * 1024 + (tt + 1) * 128],
                                            idf[:])))
                        n_t4 += 1
                        p.wait("vector", s4TT, n_t4)
                        if mb == 0:
                            p.wait("vector", s4XF, 16 * (tt + 1))
                        p.em("vector", lambda e, i=ti % 2, mb=mb, tt=tt: e.tensor_tensor(
                            ytm[:, mb * 512:(mb + 1) * 512], tr4[i][:],
                            x1ft[tt % 2][:, mb * 512:(mb + 1) * 512], op=ALU.add).then_inc(s4TE, 1))
                    if dbg >= 2 and half == 0 and tt == 0:
                        p.wait("sync", s4TE, 16)
                        p.em("sync", lambda e: e.dma_start(out=xftdbg[:], in_=x1ft[0][:]).then_inc(sDBG, 16))
                        p.em("sync", lambda e: e.dma_start(out=ytmdbg[:], in_=ytm[:]).then_inc(sDBG, 16))
                    p.em("vector", lambda e: e.reduce_sum(S1c, ytm[:], axis=AX.X))
                    p.em("vector", lambda e: e.tensor_scalar(
                        NMc, S1c, nrd_col, None, op0=ALU.mult).then_inc(s4DS, 1))
                    p.wait("scalar", s4DS, 16 + 2 * tt + 1)
                    p.wait("scalar", s4Y, 16 * (tt - 1))
                    p.em("scalar", lambda e, b=tt % 2: e.activation(
                        out=yst[b][:], in_=ytm[:], func=AF.Square, accum_out=S2c).then_inc(s4SQ, 1))
                    p.wait("vector", s4SQ, tt + 1)
                    p.em("vector", lambda e: e.tensor_tensor(M2c, NMc, NMc, op=ALU.mult))
                    p.em("vector", lambda e: e.scalar_tensor_tensor(
                        VRc, S2c, rd_col, M2c, op0=ALU.mult, op1=ALU.subtract).then_inc(s4DS, 1))
                    p.wait("scalar", s4DS, 16 + 2 * tt + 2)
                    p.em("scalar", lambda e: e.activation(
                        out=SDc, in_=VRc, func=AF.Sqrt, bias=eps_col, scale=1.0).then_inc(s4IV, 1))
                    p.wait("vector", s4IV, tt + 1)
                    p.em("vector", lambda e: e.reciprocal(IVc, SDc).then_inc(s4RC, 1))
                    p.wait("vector", s4RC, tt + 1)
                    p.em("vector", lambda e, b=tt % 2: e.tensor_scalar(
                        yst[b][:], ytm[:], NMc, IVc, op0=ALU.add, op1=ALU.mult).then_inc(s4LN, 1))
                    p.wait("sync", s4LN, tt + 1)
                    p.em("sync", lambda e, tt=tt, b=tt % 2: e.dma_start(
                        out=_ds(y_out, half * 1024 + tt * 128, 128, D),
                        in_=yst[b][:]).then_inc(s4Y, 16))
                p.wait("sync", s4Y, 16 * 8)
                p.install_into(blk4b)
            blk4b.__exit__(None, None, None)

    return nc


def _prep_inputs(kw):
    f32 = np.float32
    q = np.ascontiguousarray(np.asarray(kw["L0_attn_query"], f32))
    com = {
        "wq": np.ascontiguousarray(np.asarray(kw["L0_attn_w_q"], f32)),
        "wk": np.ascontiguousarray(np.asarray(kw["L0_attn_w_k"], f32)),
        "wv": np.ascontiguousarray(np.asarray(kw["L0_attn_w_v"], f32)),
        "wo": np.ascontiguousarray(np.asarray(kw["L0_attn_w_o"], f32)),
        "gw": np.ascontiguousarray(np.asarray(kw["L0_ffn_gate_w"], f32)),
        "uw": np.ascontiguousarray(np.asarray(kw["L0_ffn_up_w"], f32)),
        "dw": np.ascontiguousarray(np.asarray(kw["L0_ffn_down_w"], f32)),
    }
    bq = np.asarray(kw["L0_attn_b_q"], f32).reshape(KD, 128).T
    bk = np.asarray(kw["L0_attn_b_k"], f32).reshape(KD, 128).T
    bv = np.asarray(kw["L0_attn_b_v"], f32).reshape(KD, 128).T
    com["bqkv"] = np.ascontiguousarray(np.concatenate([bq, bk, bv], axis=1))
    com["bo_row"] = np.ascontiguousarray(np.asarray(kw["L0_attn_b_o"], f32).reshape(1, D))
    com["gsc"] = np.ascontiguousarray(np.asarray(kw["L0_ffn_gate_scale"], f32).reshape(KI, 128).T)
    com["usc"] = np.ascontiguousarray(np.asarray(kw["L0_ffn_up_scale"], f32).reshape(KI, 128).T)
    com["dsc"] = np.ascontiguousarray(np.asarray(kw["L0_ffn_down_scale"], f32).reshape(KD, 128).T)
    com["idf"] = np.eye(128, dtype=f32)
    com["idb"] = np.eye(128, dtype=ml_dtypes.bfloat16)
    r = np.arange(128)
    com["maskneg"] = np.where(r[:, None] >= r[None, :], 0.0, -60000.0).astype(f32)
    com["ones_row"] = np.ones((1, 128), f32)
    cv = np.zeros((128, NCONST), f32)
    cv[:, C_S] = S_ATTN
    cv[:, C_NRD] = -1.0 / D
    cv[:, C_RD] = 1.0 / D
    cv[:, C_EPS] = 1e-5
    com["cvec"] = cv
    in_maps = []
    for c in range(8):
        m = dict(com)
        m["x"] = np.ascontiguousarray(q[c])
        in_maps.append(m)
    return in_maps


def kernel(**kw):
    nc = build_nc(upto=4, dbg=False)
    in_maps = _prep_inputs(kw)
    res = run_bass_kernel_spmd(nc, in_maps, core_ids=list(range(8)))
    out = np.stack([np.asarray(r["y"], np.float32) for r in res.results], axis=0)
    return out
